# version 11
# baseline (speedup 1.0000x reference)
import os
import numpy as np
from contextlib import ExitStack

import concourse.bass as bass
import concourse.mybir as mybir
from concourse.bass_utils import run_bass_kernel_spmd

F32 = mybir.dt.float32
BF16 = mybir.dt.bfloat16
AF = mybir.ActivationFunctionType
ALU = mybir.AluOpType

B, T, D = 16, 2048, 1024
NCORES = 8
BC = B // NCORES
NT = D // 128
TT = T // 128
NF = T // 512
S = 16
PAD = 2049
PADB = 2052
H2 = 1024
DH, WIN = 64, 512
HIDR = 2730
HIDH = 2816
NSL = HIDH // 128
NL = 2
NEXP = 2


class Sched:
    ENGS = ("pe", "act", "dve", "pool", "sp")

    def __init__(self):
        self.ops = []
        self.last_w = {}
        self.readers = {}

    def emit(self, eng, fn, reads=(), writes=(), dma=False):
        opid = len(self.ops)
        deps = set()
        for r in reads:
            deps.update(self.last_w.get(r, ()))
            if r.startswith("ps"):
                for rd in self.readers.get(r, ()):
                    if self.ops[rd]["eng"] != eng:
                        deps.add(rd)
        for r in writes:
            deps.update(self.last_w.get(r, ()))
            deps.update(self.readers.get(r, ()))
        self.ops.append(dict(id=opid, eng=eng, fn=fn, dma=dma, deps=deps))
        for r in reads:
            self.readers.setdefault(r, []).append(opid)
        for r in writes:
            self.last_w[r] = [opid]
            self.readers[r] = []
        return opid

    def barrier(self, names):
        deps = set()
        for r in names:
            deps.update(self.last_w.get(r, ()))
            deps.update(self.readers.get(r, ()))
        dl = sorted(deps)
        for r in names:
            self.last_w[r] = dl
            self.readers[r] = []

    def finalize(self, nc, es):
        ops = self.ops
        d_idx = {}
        dc = {e: 0 for e in self.ENGS}
        for op in ops:
            if op["dma"]:
                dc[op["eng"]] += 1
                d_idx[op["id"]] = dc[op["eng"]]
        marked = set()
        last_drain = {e: -1 for e in self.ENGS}
        for op in ops:
            kept = []
            need_drain = False
            for d in op["deps"]:
                p = ops[d]
                same = p["eng"] == op["eng"]
                if same and not p["dma"]:
                    if op["eng"] != "pe" and d > last_drain[op["eng"]]:
                        need_drain = True
                    continue
                kept.append(d)
            if need_drain:
                last_drain[op["eng"]] = op["id"] - 1
            op["drain"] = need_drain
            op["kept"] = kept
            for d in kept:
                if not ops[d]["dma"]:
                    marked.add(d)
        mval = {}
        mc = {e: 0 for e in self.ENGS}
        for op in ops:
            if not op["dma"]:
                if op["id"] in marked:
                    mc[op["eng"]] += 1
                mval[op["id"]] = mc[op["eng"]]
        NS = self.NS = 8
        self.sem_c = {e: es.enter_context(nc.semaphore(f"sc_{e}")) for e in self.ENGS}
        self.sem_d = {e: [es.enter_context(nc.semaphore(f"sd_{e}{k}")) for k in range(NS)]
                      for e in self.ENGS}
        self.marked = marked
        for op in ops:
            if op["dma"]:
                di = d_idx[op["id"]] - 1
                op["slot"] = di % NS
                op["round"] = di // NS
            w = {}
            for d in op["kept"]:
                p = ops[d]
                if p["dma"]:
                    di = d_idx[d] - 1
                    key, v = ("d", p["eng"], di % NS), 16 * (di // NS + 1)
                else:
                    key, v = ("c", p["eng"], 0), mval[d]
                if v > w.get(key, 0):
                    w[key] = v
            op["waits"] = sorted(w.items())

    def run_engine(self, eng_name, handle):
        seen = {}
        for op in self.ops:
            if op["eng"] != eng_name:
                continue
            for (typ, pe, sl), v in op["waits"]:
                if seen.get((typ, pe, sl), -1) >= v:
                    continue
                handle.wait_ge(self.sem_d[pe][sl] if typ == "d" else self.sem_c[pe], v)
                seen[(typ, pe, sl)] = v
            if op.get("drain") and not os.environ.get("KNODRAIN"):
                handle.drain()
            if op["fn"] is None:
                continue
            if op["dma"]:
                sl, rnd = op["slot"], op["round"]
                if rnd > 0 and seen.get(("d", eng_name, sl), -1) < 16 * rnd:
                    handle.wait_ge(self.sem_d[eng_name][sl], 16 * rnd)
                    seen[("d", eng_name, sl)] = 16 * rnd
                instr = op["fn"](handle)
                instr.then_inc(self.sem_d[eng_name][sl], 16)
            else:
                instr = op["fn"](handle)
                if op["id"] in self.marked:
                    instr.then_inc(self.sem_c[eng_name], 1)


def build_program():
    nc = bass.Bass("TRN2", target_bir_lowering=False)
    dt = {}

    def din(name, shape):
        dt[name] = nc.dram_tensor(name, list(shape), F32, kind="ExternalInput").ap()
        return dt[name]

    x_in = din("x", (BC, T, D))
    y_out = nc.dram_tensor("y", [BC, T, D], F32, kind="ExternalOutput").ap()
    x1sp = nc.dram_tensor("x1sp", [BC, 128, NT * T], F32, kind="Internal").ap()
    w1c = nc.dram_tensor("w1c", [NL, NSL, 128, NT * 256], BF16, kind="Internal").ap()
    w2c = nc.dram_tensor("w2c", [NL, NSL, 128, D], BF16, kind="Internal").ap()
    KDBG = bool(os.environ.get("KDBG"))
    KCUT = int(os.environ.get("KCUT", "99"))
    KLOAD = os.environ.get("KLOAD", "full")
    KOUT = os.environ.get("KOUT", "full")
    KTT = int(os.environ.get("KTT", str(TT)))
    KB = int(os.environ.get("KB", str(BC)))
    KPRE = os.environ.get("KPRE", "full")
    dbg = nc.dram_tensor("dbg", [BC, NL, 128, NT * T], F32, kind="Internal").ap() if KDBG else None
    dbg2 = nc.dram_tensor("dbg2", [BC, NL, 128, NT * T], F32, kind="Internal").ap() if KDBG else None
    dbg3 = nc.dram_tensor("dbg3", [12, 128, T], BF16, kind="Internal").ap() if KDBG else None
    dbg4 = nc.dram_tensor("dbg4", [4, 128, T], F32, kind="Internal").ap() if KDBG else None
    dbg5 = nc.dram_tensor("dbg5", [4, 128, 512], F32, kind="Internal").ap() if KDBG else None
    dbg6 = nc.dram_tensor("dbg6", [4, 1, 512], F32, kind="Internal").ap() if KDBG else None

    din("wdt", (NT, 128, NT * 128))
    din("wot", (NT, 128, NT * 128))
    din("qkvw", (NL, 12, 128, NT * 128))
    din("wow", (NL, NT, 128, 512))
    din("w1t", (NL, NSL, 128, NT * 256))
    din("w2t", (NL, NSL, 128, D))
    for nm, sh in [("alog", (128, 128)), ("bpv", (128, 128)), ("bdv", (128, NT)),
                   ("bov", (128, NT)), ("b1av", (128, NL * NSL)), ("b1gv", (128, NL * NSL)),
                   ("b2v", (128, NL * NT)), ("lgv", (128, NL * NT)), ("lbv", (128, NL * NT)),
                   ("qsv", (128, NL)), ("ksv", (128, NL)),
                   ("rotq_c", (128, T)), ("rotq_s", (128, T)),
                   ("rotk_c", (128, T)), ("rotk_s", (128, T)),
                   ("mask_g0", (128, 128)), ("mask_g4", (128, 128)),
                   ("identf", (128, 128)), ("epsv", (128, 8)), ("rep2c", (2, 128))]:
        din(nm, sh)

    es = ExitStack()
    sc = Sched()
    E = sc.emit
    BAR = sc.barrier
    with es:
        sb = lambda n, p, f, d=F32: es.enter_context(nc.sbuf_tensor(n, [p, f], d))
        X = sb("X", 128, NT * PAD)
        XB = sb("XB", 128, NT * PADB, BF16)
        AR = sb("AR", 128, 21248)
        ar = AR.ap()
        arb = ar.bitcast(BF16)
        delta = ar[:, 0:2048]
        u0 = ar[:, 2048:4096]
        abar = ar[:, 4096:6144]
        statb = ar[:, 6144:8192]
        yac = ar[:, 8192:10240]
        ABUF = ar[:, 0:2048]
        BBUF = ar[:, 2048:4096]
        CBUF = ar[:, 4096:6144]
        VT = arb[:, 12288:14336]
        EXPT = arb[:, 14336:14336 + NEXP * 512]
        TS2 = ar[:, 7680:8192]
        GEL = ar[:, 7680:8192]
        rq_c = arb[:, 16384:18432]
        rq_s = arb[:, 18432:20480]

        HBv = arb[:, 0:4096]
        ZB = arb[:, 4096:4096 + NSL * 512]
        QKY = arb[:, 20480:36864]
        TS1 = ar[:, 18432:18944]
        SQB = arb[:, 37888:38400]
        MUB = ar[0:1, 19200:19712]
        VARB = ar[0:1, 19712:20224]
        INVB = ar[0:1, 20224:20736]
        RCD = ar[0:1, 20736:21248]
        RSQ = ar[0:2, 20736:21248]
        ABAR2 = sb("ABAR2", 128, 1024)
        XCOL = sb("XCOL", 128, NT)
        XCOLB = sb("XCOLB", 128, NT, BF16)
        WD1 = sb("WD1", 128, 4096, BF16)
        WD2 = sb("WD2", 128, 2048, BF16)
        WSS = WD1.ap()[:, 0:2048]
        WOB = WD1.ap()[:, 2048:3072]
        WQK = WD2.ap()
        navec = sb("navec", 128, 128)
        bpv = sb("bpv_s", 128, 128)
        bdv = sb("bdv_s", 128, NT)
        bov = sb("bov_s", 128, NT)
        b1av = sb("b1av_s", 128, NL * NSL)
        b1gv = sb("b1gv_s", 128, NL * NSL)
        b2v = sb("b2v_s", 128, NL * NT)
        lgv = sb("lgv_s", 128, NL * NT)
        lbv = sb("lbv_s", 128, NL * NT)
        qsv = sb("qsv_s", 128, NL)
        ksv = sb("ksv_s", 128, NL)
        rk_c = sb("rk_c_s", 128, T, BF16)
        rk_s = sb("rk_s_s", 128, T, BF16)
        m_g0 = sb("m_g0_s", 128, 128, BF16)
        m_g4 = sb("m_g4_s", 128, 128, BF16)
        identf = sb("identf_s", 128, 128)
        epsv = sb("epsv_s", 128, 8)
        onescol = sb("onescol", 128, 1, BF16)
        onesrow = sb("onesrow", 1, 128)
        sel64 = sb("sel64", 128, 2, BF16)
        rep2 = sb("rep2", 2, 128)
        alog_t = ar[:, 0:128]
        ps = [es.enter_context(nc.psum_tensor(f"ps{i}", [128, 512], F32)) for i in range(8)]

        X3 = X.ap().rearrange("p (t c) -> p t c", c=PAD)
        XB3 = XB.ap().rearrange("p (t c) -> p t c", c=PADB)

        SCAN_N = ["delta", "u0", "abb0", "abb1", "st0", "st1",
                  "y0", "y1", "wss0", "wss1", "c_alog"]
        ATTN_N = ["abuf", "bbuf", "cb0", "cb1", "VT0", "ts2", "c_rotq_c", "c_rotq_s"] + \
                 [f"exs{k}" for k in range(NEXP)] + ["wob0", "wob1", "wqk0", "wqk1"]
        W12_N = [f"HB{i}" for i in range(NT)] + [f"ZB{s}" for s in range(NSL)] + \
                ["gel", "wd10", "wd11", "wd20", "wd21"]
        ALL_AR = sorted(set(SCAN_N + ATTN_N + W12_N))

        def xsl(i, t0, n, sh=0):
            o = i * PAD + 1 - sh + t0
            return X[:, o:o + n]

        def xbsl(i, t0, n, sh=0):
            o = i * PADB + 2 - sh + t0
            return XB[:, o:o + n]

        def hx(i, t0, n):
            return xsl(i, t0, n, sh=1 if i >= 4 else 0)

        def hxb(i, t0, n):
            return xbsl(i, t0, n, sh=1 if i >= 4 else 0)

        _cl = [("bdv", bdv), ("bov", bov), ("b1av", b1av), ("b1gv", b1gv),
               ("b2v", b2v), ("lgv", lgv), ("lbv", lbv), ("qsv", qsv),
               ("ksv", ksv),
               ("identf", identf), ("bpv", bpv), ("alog", alog_t),
               ("epsv", None)]
        for nm, dst in [("mask_g0", m_g0), ("mask_g4", m_g4)]:
            E("pool", (lambda d, s: lambda h: h.dma_start(out=d[:, :], in_=s))(dst, dt[nm]),
              writes=[f"c_{nm}"], dma=True)
        if KPRE == "min":
            _cl = [("identf", identf), ("bpv", bpv), ("alog", alog_t), ("epsv", None)]
        for nm, dst in _cl:
            if dst is None:
                dst = epsv
            E("sp", (lambda d, s: lambda h: h.dma_start(out=d[:, :], in_=s))(dst, dt[nm]),
              writes=[f"c_{nm}"], dma=True)
        if KPRE != "min":
            for nm, dst in [("rotk_c", rk_c), ("rotk_s", rk_s)]:
                E("pool", (lambda d, s: lambda h: h.dma_start(out=d[:, :], in_=s))(dst, dt[nm]),
                  writes=[f"c_{nm}"], dma=True)
        def emit_small_memsets():
            E("dve", lambda h: h.memset(onescol[:, :], 1.0), writes=["c_ones"])
            E("dve", lambda h: h.memset(onesrow[:, :], 1.0), writes=["c_onesr"])
            E("dve", lambda h: h.memset(sel64[:, :], 0.0), writes=["c_sel"])
            E("dve", lambda h: h.memset(sel64[0:64, 0:1], 1.0), writes=["c_sel"])
            E("dve", lambda h: h.memset(sel64[64:128, 1:2], 1.0), writes=["c_sel"])
            E("sp", lambda h: h.dma_start(out=rep2[:, :], in_=dt["rep2c"]),
              writes=["c_rep2"], dma=True)
        if KPRE != "nomemset":
            emit_small_memsets()
        E("act", lambda h: h.activation(navec[:, :], alog_t[:, :], AF.Abs),
          reads=["c_alog"], writes=["c_nav"])
        E("act", lambda h: h.activation(navec[:, :], navec[:, :], AF.Exp, scale=epsv[:, 4:5]),
          reads=["c_nav", "c_epsv"], writes=["c_nav"])
        E("act", lambda h: h.activation(navec[:, :], navec[:, :], AF.Ln, bias=1.0),
          reads=["c_nav"], writes=["c_nav"])
        E("act", lambda h: h.activation(abar[:, 0:128], alog_t[:, :], AF.Relu),
          reads=["c_alog"], writes=["abb0"])
        E("dve", lambda h: h.tensor_add(navec[:, :], navec[:, :], abar[:, 0:128]),
          reads=["abb0", "c_nav"], writes=["c_nav"])
        E("dve", lambda h: h.tensor_scalar(navec[:, :], navec[:, :], epsv[:, 4:5], epsv[:, 2:3], ALU.mult, ALU.add),
          reads=["c_nav", "c_epsv"], writes=["c_nav"])
        E("dve", lambda h: h.memset(X3[:, :, 0:1], 0.0), writes=[f"X{i}" for i in range(NT)])
        E("dve", lambda h: h.memset(XB3[:, :, 0:2], 0.0), writes=[f"XB{i}" for i in range(NT)])

        allXB = [f"XB{i}" for i in range(NT)]

        def ln_stats(rhs_list):
            n = len(rhs_list)
            for k, (ap_, nn, off) in enumerate(rhs_list):
                E("pe", (lambda ap_=ap_, k=k, n=n, nn=nn, off=off: lambda h: h.matmul(
                    ps[6][0:1, off:off + nn], onescol[:, :], ap_,
                    start=(k == 0), stop=(k == n - 1)))(),
                    reads=["c_ones"] + allXB + ["xcolb"], writes=["ps6"])
            E("act", lambda h: h.activation(MUB, ps[6][0:1, :], AF.Copy, scale=epsv[0:1, 3:4]),
              reads=["ps6", "c_epsv"], writes=["mu"])
            for k, (ap_, nn, off) in enumerate(rhs_list):
                if nn == 512:
                    E("act", (lambda ap_=ap_: lambda h: h.activation(
                        SQB[:, 0:512], ap_, AF.Square))(),
                        reads=allXB + ["xcolb"], writes=["sqb"])
                else:
                    E("dve", (lambda ap_=ap_, nn=nn, off=off: lambda h: h.tensor_mul(
                        SQB[:, off:off + nn], ap_, ap_))(),
                        reads=allXB + ["xcolb"], writes=["sqb"])
                E("pe", (lambda k=k, nn=nn, off=off, n=n: lambda h: h.matmul(
                    ps[7][0:1, off:off + nn], onescol[:, :], SQB[:, off:off + nn],
                    start=(k == 0), stop=(k == n - 1)))(),
                    reads=["c_ones", "sqb"], writes=["ps7"])
            E("act", lambda h: h.activation(VARB, ps[7][0:1, :], AF.Copy, scale=epsv[0:1, 3:4]),
              reads=["ps7", "c_epsv"], writes=["var"])
            E("dve", lambda h: h.tensor_mul(RCD, MUB, MUB), reads=["mu"], writes=["rcd"])
            E("dve", lambda h: h.tensor_sub(VARB, VARB, RCD),
              reads=["var", "rcd"], writes=["var"])
            E("act", lambda h: h.activation(INVB, VARB, AF.Ln, bias=epsv[0:1, 0:1]),
              reads=["var", "c_epsv"], writes=["inv"])
            E("act", lambda h: h.activation(INVB, INVB, AF.Exp, scale=epsv[0:1, 5:6]),
              reads=["inv", "c_epsv"], writes=["inv"])
            E("dve", lambda h: h.tensor_mul(MUB, MUB, INVB),
              reads=["mu", "inv"], writes=["mu"])

        def ln_replicate():
            E("pe", lambda h: h.matmul(ps[6][:, :], onesrow[:, :], INVB, start=True, stop=True),
              reads=["inv", "c_onesr"], writes=["ps6"])
            E("pe", lambda h: h.matmul(ps[7][:, :], onesrow[:, :], MUB, start=True, stop=True),
              reads=["mu", "c_onesr"], writes=["ps7"])

        def plain_rhs(f):
            return [(xbsl(i, f * 512, 512), 512, 0) for i in range(NT)]

        def ln_plain(f):
            ln_stats(plain_rhs(f))
            ln_replicate()
            t0 = f * 512
            for i in range(NT):
                E("dve", (lambda i=i, t0=t0: lambda h: h.tensor_mul(
                    xsl(i, t0, 512), xsl(i, t0, 512), ps[6][:, :]))(),
                    reads=["ps6", f"X{i}"], writes=[f"X{i}"])
                E("dve", (lambda i=i, t0=t0: lambda h: h.tensor_sub(
                    xsl(i, t0, 512), xsl(i, t0, 512), ps[7][:, :]))(),
                    reads=["ps7", f"X{i}"], writes=[f"X{i}"])
                E("act", (lambda i=i, t0=t0: lambda h: h.activation(
                    xbsl(i, t0, 512), xsl(i, t0, 512), AF.Copy))(),
                    reads=[f"X{i}"], writes=[f"XB{i}"])

        def phase_load(b):
            for tt in range(KTT):
                cb = tt % 2
                E("sp", (lambda tt=tt, cb=cb: lambda h: h.dma_start(
                    out=CBUF[:, cb * 1024:(cb + 1) * 1024],
                    in_=x_in[b, tt * 128:(tt + 1) * 128, :]))(),
                    writes=[f"cb{cb}"], dma=True)
                if KLOAD == "dma":
                    continue
                for half in range(2):
                    pb = (4 if cb == 0 else 6) + half
                    for k in range(4):
                        i = half * 4 + k
                        E("pe", (lambda cb=cb, i=i, k=k, pb=pb: lambda h: h.transpose(
                            ps[pb][:, k * 128:(k + 1) * 128],
                            CBUF[:, cb * 1024 + i * 128: cb * 1024 + (i + 1) * 128],
                            identf[:, :]))(),
                            reads=[f"cb{cb}", "c_identf"], writes=[f"ps{pb}"])
                    px3 = ps[pb].ap().rearrange("p (t c) -> p t c", c=128)
                    E("dve", (lambda tt=tt, half=half, px3=px3: lambda h: h.tensor_copy(
                        X3[:, half * 4:(half + 1) * 4, 1 + tt * 128: 1 + (tt + 1) * 128],
                        px3[:, :, :]))(),
                        reads=[f"ps{pb}"], writes=[f"X{half*4+k}" for k in range(4)])
                    for k in range(4):
                        i2 = half * 4 + k
                        E("act", (lambda tt=tt, i2=i2, k=k, pb=pb: lambda h: h.activation(
                            xbsl(i2, tt * 128, 128),
                            ps[pb][:, k * 128:(k + 1) * 128], AF.Copy))(),
                            reads=[f"ps{pb}"], writes=[f"XB{i2}"])

        def phase_ssm(b):
            for i in range(NT):
                slot = i % 2
                E("pool", (lambda i=i, slot=slot: lambda h: h.dma_start(
                    out=WSS[:, slot * 1024:(slot + 1) * 1024], in_=dt["wdt"][i]))(),
                    writes=[f"wss{slot}"], dma=True)
                for f in range(NF):
                    pb = f % 4
                    for pi in range(NT):
                        E("pe", (lambda i=i, f=f, pi=pi, slot=slot, pb=pb: lambda h: h.matmul(
                            ps[pb][:, :],
                            WSS[:, slot * 1024 + pi * 128: slot * 1024 + (pi + 1) * 128],
                            xbsl(pi, f * 512, 512),
                            start=(pi == 0), stop=(pi == NT - 1)))(),
                            reads=[f"wss{slot}", f"XB{pi}"], writes=[f"ps{pb}"])
                    E("act", (lambda i=i, pb=pb: lambda h: h.activation(
                        TS1[:, :], ps[pb][:, :], AF.Abs, bias=bdv[:, i:i + 1]))(),
                        reads=[f"ps{pb}", "c_bdv"], writes=["ts1"])
                    E("act", (lambda i=i, f=f, pb=pb: lambda h: h.activation(
                        delta[:, f * 512:(f + 1) * 512], ps[pb][:, :], AF.Relu,
                        bias=bdv[:, i:i + 1]))(),
                        reads=[f"ps{pb}", "c_bdv"], writes=["delta"])
                    E("act", lambda h: h.activation(TS1[:, :], TS1[:, :], AF.Exp, scale=epsv[:, 4:5]),
                      reads=["ts1", "c_epsv"], writes=["ts1"])
                    E("act", lambda h: h.activation(TS1[:, :], TS1[:, :], AF.Ln, bias=1.0),
                      reads=["ts1"], writes=["ts1"])
                    E("dve", (lambda f=f: lambda h: h.tensor_add(
                        delta[:, f * 512:(f + 1) * 512], delta[:, f * 512:(f + 1) * 512],
                        TS1[:, :]))(),
                        reads=["ts1", "delta"], writes=["delta"])
                E("pool", (lambda i=i: lambda h: h.tensor_mul(
                    u0[:, :], delta[:, :], xsl(i, 0, T)))(),
                    reads=["delta", f"X{i}"], writes=["u0"])
                bufs = [(abar[:, 0:H2], "abb0"), (abar[:, H2:2 * H2], "abb1"),
                        (ABAR2.ap()[:, 0:H2], "abb2")]
                for j in range(S):
                    c = i * 16 + j
                    for hf in range(2):
                        ab, abn = bufs[(2 * j + hf) % 3]
                        E("act", (lambda c=c, hf=hf, ab=ab: lambda h: h.activation(
                            ab, delta[:, hf * H2:(hf + 1) * H2],
                            AF.Exp, scale=navec[:, c:c + 1]))(),
                            reads=["delta", "c_nav"], writes=[abn])
                        init = 0.0 if hf == 0 else statb[:, H2 - 1:H2]
                        E("dve", (lambda hf=hf, init=init, ab=ab: lambda h: h.tensor_tensor_scan(
                            statb[:, hf * H2:(hf + 1) * H2], ab,
                            u0[:, hf * H2:(hf + 1) * H2], init, ALU.mult, ALU.add))(),
                            reads=[abn, "u0"] + (["st0"] if hf else []),
                            writes=[f"st{hf}"])
                        if j == 0:
                            E("dve", (lambda c=c, hf=hf: lambda h: h.tensor_scalar_mul(
                                yac[:, hf * H2:(hf + 1) * H2], statb[:, hf * H2:(hf + 1) * H2],
                                bpv[:, c:c + 1]))(),
                                reads=[f"st{hf}", "c_bpv"], writes=[f"y{hf}"])
                        else:
                            E("dve", (lambda c=c, hf=hf: lambda h: h.scalar_tensor_tensor(
                                yac[:, hf * H2:(hf + 1) * H2], statb[:, hf * H2:(hf + 1) * H2],
                                bpv[:, c:c + 1], yac[:, hf * H2:(hf + 1) * H2],
                                ALU.mult, ALU.add))(),
                                reads=[f"st{hf}", "c_bpv", f"y{hf}"], writes=[f"y{hf}"])
                E("act", (lambda i=i: lambda h: h.activation(
                    QKY[:, i * T:(i + 1) * T], yac[:, :], AF.Copy))(),
                    reads=["y0", "y1"], writes=[f"QK{i}"])
            for e in range(NT):
                slot = e % 2
                E("pool", (lambda e=e, slot=slot: lambda h: h.dma_start(
                    out=WSS[:, slot * 1024:(slot + 1) * 1024], in_=dt["wot"][e]))(),
                    writes=[f"wss{slot}"], dma=True)
                for f in range(NF):
                    pb = f % 4
                    for pi in range(NT):
                        E("pe", (lambda e=e, f=f, pi=pi, slot=slot, pb=pb: lambda h: h.matmul(
                            ps[pb][:, :],
                            WSS[:, slot * 1024 + pi * 128: slot * 1024 + (pi + 1) * 128],
                            QKY[:, pi * T + f * 512: pi * T + (f + 1) * 512],
                            start=(pi == 0), stop=(pi == NT - 1)))(),
                            reads=[f"wss{slot}", f"QK{pi}"], writes=[f"ps{pb}"])
                    E("dve", (lambda e=e, f=f, pb=pb: lambda h: h.scalar_tensor_tensor(
                        xsl(e, f * 512, 512), ps[pb][:, :], bov[:, e:e + 1],
                        xsl(e, f * 512, 512), ALU.add, ALU.add))(),
                        reads=[f"ps{pb}", "c_bov", f"X{e}"], writes=[f"X{e}"])
                    E("act", (lambda e=e, f=f: lambda h: h.activation(
                        xbsl(e, f * 512, 512), xsl(e, f * 512, 512), AF.Copy))(),
                        reads=[f"X{e}"], writes=[f"XB{e}"])
            for f in range(NF):
                ln_plain(f)
            for i in range(NT):
                E("sp", (lambda b=b, i=i: lambda h: h.dma_start(
                    out=x1sp[b, :, i * T:(i + 1) * T], in_=xsl(i, 0, T)))(),
                    reads=[f"X{i}"], writes=[f"sp{i}"], dma=True)

        def qk_norm_rot(L, pt, kind):
            dtile = pt if kind == "q" else 4 + (pt - 4)
            scv = qsv if kind == "q" else ksv
            rc = rq_c if kind == "q" else rk_c
            rs = rq_s if kind == "q" else rk_s
            rcn = "c_rotq_c" if kind == "q" else "c_rotk_c"
            rsn = "c_rotq_s" if kind == "q" else "c_rotk_s"
            for f in range(NF):
                E("act", (lambda f=f: lambda h: h.activation(
                    SQB[:, :], ABUF[:, f * 512:(f + 1) * 512], AF.Square))(),
                    reads=["abuf"], writes=["sqb"])
                E("pe", lambda h: h.matmul(ps[2][0:2, :], sel64[:, :], SQB[:, :],
                                           start=True, stop=True),
                  reads=["sqb", "c_sel"], writes=["ps2"])
                E("act", lambda h: h.activation(RSQ, ps[2][0:2, :], AF.Ln, bias=epsv[0:2, 1:2]),
                  reads=["ps2", "c_epsv"], writes=["rcd"])
                E("act", lambda h: h.activation(RSQ, RSQ, AF.Exp, scale=epsv[0:2, 5:6]),
                  reads=["rcd", "c_epsv"], writes=["rcd"])
                E("pe", lambda h: h.matmul(ps[3][:, :], rep2[:, :], RSQ,
                                           start=True, stop=True),
                  reads=["rcd", "c_rep2"], writes=["ps3"])
                E("dve", (lambda f=f, L=L, scv=scv: lambda h: h.scalar_tensor_tensor(
                    BBUF[:, f * 512:(f + 1) * 512], ABUF[:, f * 512:(f + 1) * 512],
                    scv[:, L:L + 1], ps[3][:, :], ALU.mult, ALU.mult))(),
                    reads=["abuf", "ps3", "c_qsv", "c_ksv"], writes=["bbuf"])
            for blk in range(4):
                src = blk ^ 1
                E("sp", (lambda blk=blk, src=src: lambda h: h.dma_start(
                    out=ABUF[blk * 32:(blk + 1) * 32, :],
                    in_=BBUF[src * 32:(src + 1) * 32, :]))(),
                    reads=["bbuf"], writes=["abuf"], dma=True)
            for f in range(NF):
                E("dve", (lambda f=f, rc=rc: lambda h: h.tensor_mul(
                    TS1[:, :], BBUF[:, f * 512:(f + 1) * 512], rc[:, f * 512:(f + 1) * 512]))(),
                    reads=["bbuf", rcn], writes=["ts1"])
                E("pool", (lambda f=f, rs=rs: lambda h: h.tensor_mul(
                    TS2[:, :], ABUF[:, f * 512:(f + 1) * 512], rs[:, f * 512:(f + 1) * 512]))(),
                    reads=["abuf", rsn], writes=["ts2"])
                E("dve", (lambda f=f, dtile=dtile: lambda h: h.tensor_add(
                    QKY[:, dtile * T + f * 512: dtile * T + (f + 1) * 512],
                    TS1[:, :], TS2[:, :]))(),
                    reads=["ts1", "ts2"], writes=[f"QK{dtile}"])

        def expt_sl(slot, a, b):
            return EXPT[:, slot * 512 + a: slot * 512 + b]

        def attention_head(L, hd, dump=False):
            hb = (hd % 2) * 64
            qt = hd // 2
            for wi in range(4):
                jts = [jt for jt in range(8) if (wi > 0 or jt >= 4)]
                for n, jt in enumerate(jts):
                    its = [it for it in range(4) if 0 <= it - jt + 4 <= 4]
                    c0, c1 = min(its), max(its) + 1
                    pb = [4, 5, 2][n % 3]
                    slot = n % NEXP
                    koff = (wi - 1) * 512 + jt * 128
                    E("pe", (lambda qt=qt, hb=hb, wi=wi, koff=koff, c0=c0, c1=c1, pb=pb: lambda h: h.matmul(
                        ps[pb][:, c0 * 128:c1 * 128],
                        QKY[hb:hb + 64, (4 + qt) * T + koff: (4 + qt) * T + koff + 128],
                        QKY[hb:hb + 64, qt * T + wi * 512 + c0 * 128: qt * T + wi * 512 + c1 * 128],
                        start=True, stop=True))(),
                        reads=[f"QK{4 + qt}", f"QK{qt}"], writes=[f"ps{pb}"])
                    for it in its:
                        g = it - jt + 4
                        if g == 0:
                            E("dve", (lambda it=it, pb=pb: lambda h: h.tensor_add(
                                ps[pb][:, it * 128:(it + 1) * 128],
                                ps[pb][:, it * 128:(it + 1) * 128], m_g0[:, :]))(),
                                reads=[f"ps{pb}", "c_mask_g0"], writes=[f"ps{pb}"])
                        elif g == 4:
                            E("dve", (lambda it=it, pb=pb: lambda h: h.tensor_add(
                                ps[pb][:, it * 128:(it + 1) * 128],
                                ps[pb][:, it * 128:(it + 1) * 128], m_g4[:, :]))(),
                                reads=[f"ps{pb}", "c_mask_g4"], writes=[f"ps{pb}"])
                    if c0 > 0:
                        E("dve", (lambda slot=slot, c0=c0: lambda h: h.memset(
                            expt_sl(slot, 0, c0 * 128), 0.0))(),
                            writes=[f"exs{slot}"])
                    if c1 < 4:
                        E("dve", (lambda slot=slot, c1=c1: lambda h: h.memset(
                            expt_sl(slot, c1 * 128, 512), 0.0))(),
                            writes=[f"exs{slot}"])
                    E("act", (lambda slot=slot, c0=c0, c1=c1, pb=pb: lambda h: h.activation(
                        expt_sl(slot, c0 * 128, c1 * 128),
                        ps[pb][:, c0 * 128:c1 * 128], AF.Exp))(),
                        reads=[f"ps{pb}"], writes=[f"exs{slot}"])
                    ja = koff // 128
                    E("pe", (lambda ja=ja, slot=slot, n=n, last=(n == len(jts) - 1): lambda h: h.matmul(
                        ps[6][:, :], VT[:, ja * 128:(ja + 1) * 128],
                        expt_sl(slot, 0, 512),
                        start=(n == 0), stop=last))(),
                        reads=["VT0", f"exs{slot}"], writes=["ps6"])
                if hb == 0:
                    E("act", lambda h: h.activation(TS2[64:65, :], ps[6][64:65, :], AF.Copy),
                      reads=["ps6"], writes=["ts2"])
                    E("sp", lambda h: h.dma_start(out=RCD, in_=TS2[64:65, :]),
                      reads=["ts2"], writes=["rcd"], dma=True)
                else:
                    E("act", lambda h: h.activation(RCD, ps[6][0:1, :], AF.Copy),
                      reads=["ps6"], writes=["rcd"])
                E("act", lambda h: h.activation(RCD, RCD, AF.Ln), reads=["rcd"], writes=["rcd"])
                E("act", lambda h: h.activation(RCD, RCD, AF.Exp, scale=epsv[0:1, 4:5]),
                  reads=["rcd", "c_epsv"], writes=["rcd"])
                E("pe", lambda h: h.matmul(ps[7][:, :], onesrow[:, :], RCD,
                                           start=True, stop=True),
                  reads=["rcd", "c_onesr"], writes=["ps7"])
                E("act", (lambda hb=hb: lambda h: h.activation(
                    TS1[hb:hb + 64, :], ps[6][hb:hb + 64, :], AF.Copy))(),
                    reads=["ps6"], writes=["ts1"])
                if dump:
                    E("sp", (lambda hb=hb, wi=wi: lambda h: h.dma_start(
                        out=dbg5[wi, hb:hb + 64, :], in_=TS1[hb:hb + 64, :]))(),
                        reads=["ts1"], writes=[f"d5{wi}"], dma=True)
                    E("sp", (lambda wi=wi: lambda h: h.dma_start(
                        out=dbg6[wi], in_=RCD))(),
                        reads=["rcd"], writes=[f"d6{wi}"], dma=True)
                E("dve", (lambda qt=qt, hb=hb, wi=wi: lambda h: h.tensor_mul(
                    QKY[hb:hb + 64, qt * T + wi * 512: qt * T + (wi + 1) * 512],
                    TS1[hb:hb + 64, :], ps[7][hb:hb + 64, :]))(),
                    reads=["ts1", "ps7"], writes=[f"QK{qt}"])

        def phase_layer(b, L):
            BAR(ALL_AR)
            E("pool", lambda h: h.dma_start(out=rq_c[:, :], in_=dt["rotq_c"]),
              writes=["c_rotq_c"], dma=True)
            E("pool", lambda h: h.dma_start(out=rq_s[:, :], in_=dt["rotq_s"]),
              writes=["c_rotq_s"], dma=True)
            for pt in range(12):
                kind = "q" if pt < 4 else ("k" if pt < 8 else "v")
                slot = pt % 2
                E("pool", (lambda L=L, pt=pt, slot=slot: lambda h: h.dma_start(
                    out=WQK[:, slot * 1024:(slot + 1) * 1024], in_=dt["qkvw"][L, pt]))(),
                    writes=[f"wqk{slot}"], dma=True)
                for f in range(NF):
                    pb = f % 4
                    for pi in range(NT):
                        E("pe", (lambda pt=pt, f=f, pi=pi, slot=slot, pb=pb: lambda h: h.matmul(
                            ps[pb][:, :],
                            WQK[:, slot * 1024 + pi * 128: slot * 1024 + (pi + 1) * 128],
                            hxb(pi, f * 512, 512),
                            start=(pi == 0), stop=(pi == NT - 1)))(),
                            reads=[f"wqk{slot}", f"XB{pi}"], writes=[f"ps{pb}"])
                    if kind == "v":
                        E("act", (lambda f=f, pb=pb: lambda h: h.activation(
                            CBUF[:, f * 512:(f + 1) * 512], ps[pb][:, :], AF.Copy))(),
                            reads=[f"ps{pb}"], writes=[f"cb{f // 2}"])
                    else:
                        E("act", (lambda f=f, pb=pb: lambda h: h.activation(
                            ABUF[:, f * 512:(f + 1) * 512], ps[pb][:, :], AF.Copy))(),
                            reads=[f"ps{pb}"], writes=["abuf"])
                if kind == "v" and KDBG and b == 0 and L == 0:
                    E("sp", (lambda pt=pt: lambda h: h.dma_start(
                        out=dbg4[pt - 8], in_=CBUF[:, :]))(),
                        reads=["cb0", "cb1"], writes=[f"dbg4{pt}"], dma=True)
                if kind != "v":
                    qk_norm_rot(L, pt, kind)
                    if KDBG and b == 0 and L == 0:
                        dti = pt if kind == "q" else 4 + (pt - 4)
                        E("sp", (lambda dti=dti: lambda h: h.dma_start(
                            out=dbg3[dti], in_=QKY[:, dti * T:(dti + 1) * T]))(),
                            reads=[f"QK{dti}"], writes=[f"dbg3{dti}"], dma=True)
                else:
                    for ho in range(2):
                        hd = (pt - 8) * 2 + ho
                        E("dve", lambda h: h.memset(VT[:, :], 0.0), writes=["VT0"])
                        vt3 = VT.rearrange("p (t c) -> p t c", c=128)
                        oc = 64 if ho == 0 else 0
                        E("dve", (lambda vt3=vt3, oc=oc: lambda h: h.memset(
                            vt3[:, :, oc:oc + 1], 1.0))(), writes=["VT0"])
                        for grp in range(2):
                            pbt = 0 + grp
                            for k8 in range(8):
                                jt = grp * 8 + k8
                                E("pe", (lambda ho=ho, jt=jt, k8=k8, pbt=pbt: lambda h: h.transpose(
                                    ps[pbt][:, k8 * 64:(k8 + 1) * 64],
                                    CBUF[ho * 64:(ho + 1) * 64, jt * 128:(jt + 1) * 128],
                                    identf[ho * 64:(ho + 1) * 64, ho * 64:(ho + 1) * 64]))(),
                                    reads=["cb0", "cb1", "c_identf"], writes=[f"ps{pbt}"])
                            pv3 = ps[pbt].ap().rearrange("p (t c) -> p t c", c=64)
                            vt64 = VT.rearrange("p (t c) -> p t c", c=64)
                            du = 0 if ho == 0 else 1
                            E("dve", (lambda grp=grp, du=du, pv3=pv3, vt64=vt64: lambda h: h.tensor_copy(
                                vt64[:, grp * 16 + du: grp * 16 + 16: 2, :], pv3[:, :, :]))(),
                                reads=[f"ps{pbt}"], writes=["VT0"])
                        attention_head(L, hd, dump=(KDBG and b == 0 and L == 0 and hd == 0))
            if KDBG and b == 0 and L == 0:
                for qt_ in range(4):
                    E("sp", (lambda qt_=qt_: lambda h: h.dma_start(
                        out=dbg3[8 + qt_], in_=QKY[:, qt_ * T:(qt_ + 1) * T]))(),
                        reads=[f"QK{qt_}"], writes=[f"dbg3o{qt_}"], dma=True)
            for po in range(NT):
                slot = po % 2
                E("pool", (lambda L=L, po=po, slot=slot: lambda h: h.dma_start(
                    out=WOB[:, slot * 512:(slot + 1) * 512], in_=dt["wow"][L, po]))(),
                    writes=[f"wob{slot}"], dma=True)
                for f in range(NF):
                    pb = (po % 2) * 2 + (f % 2)
                    for pt2 in range(4):
                        E("pe", (lambda pt2=pt2, po=po, f=f, pb=pb, slot=slot: lambda h: h.matmul(
                            ps[pb][:, :],
                            WOB[:, slot * 512 + pt2 * 128: slot * 512 + (pt2 + 1) * 128],
                            QKY[:, pt2 * T + f * 512: pt2 * T + (f + 1) * 512],
                            start=(pt2 == 0), stop=(pt2 == 3)))(),
                            reads=[f"wob{slot}", f"QK{pt2}"], writes=[f"ps{pb}"])
                    E("dve", (lambda po=po, f=f, pb=pb: lambda h: h.tensor_add(
                        xsl(po, f * 512, 512), xsl(po, f * 512, 512), ps[pb][:, :]))(),
                        reads=[f"ps{pb}", f"X{po}"], writes=[f"X{po}"])
                    E("act", (lambda po=po, f=f: lambda h: h.activation(
                        xbsl(po, f * 512, 512), xsl(po, f * 512, 512), AF.Copy))(),
                        reads=[f"X{po}"], writes=[f"XB{po}"])
            if KDBG:
                for i in range(NT):
                    E("sp", (lambda b=b, L=L, i=i: lambda h: h.dma_start(
                        out=dbg2[b, L, :, i * T:(i + 1) * T], in_=xsl(i, 0, T)))(),
                        reads=[f"X{i}"], writes=[f"dbg2{i}"], dma=True)
            BAR(ALL_AR)
            for f in range(NF):
                t0 = f * 512
                rl = []
                for i in range(NT):
                    if i < 4 or f == 0:
                        rl.append((hxb(i, t0, 512), 512, 0))
                    else:
                        rl.append((XCOLB.ap()[:, i:i + 1], 1, 0))
                        rl.append((xbsl(i, t0, 511), 511, 1))
                ln_stats(rl)
                ln_replicate()
                for i in range(NT):
                    if i < 4 or f == 0:
                        E("dve", (lambda i=i, t0=t0: lambda h: h.tensor_mul(
                            TS1[:, :], hx(i, t0, 512), ps[6][:, :]))(),
                            reads=["ps6", f"X{i}", "xcol"], writes=["ts1"])
                    else:
                        E("dve", (lambda i=i: lambda h: h.tensor_mul(
                            TS1[:, 0:1], XCOL.ap()[:, i:i + 1], ps[6][:, 0:1]))(),
                            reads=["ps6", "xcol"], writes=["ts1"])
                        E("dve", (lambda i=i, t0=t0: lambda h: h.tensor_mul(
                            TS1[:, 1:512], xsl(i, t0, 511), ps[6][:, 1:512]))(),
                            reads=["ps6", f"X{i}"], writes=["ts1"])
                    E("dve", lambda h: h.tensor_sub(TS1[:, :], TS1[:, :], ps[7][:, :]),
                      reads=["ps7", "ts1"], writes=["ts1"])
                    E("dve", (lambda i=i, g=L * NT + i: lambda h: h.tensor_scalar(
                        HBv[:, i * 512:(i + 1) * 512], TS1[:, :],
                        lgv[:, g:g + 1], lbv[:, g:g + 1], ALU.mult, ALU.add))(),
                        reads=["ts1", "c_lgv", "c_lbv"], writes=[f"HB{i}"])
                E("dve", (lambda t0=t0: lambda h: h.tensor_copy(
                    XCOL.ap()[:, 0:NT], X3[:, :, 1 + t0 + 511: 2 + t0 + 511]))(),
                    reads=[f"X{i}" for i in range(NT)], writes=["xcol"])
                E("act", (lambda t0=t0: lambda h: h.activation(
                    XCOLB.ap()[:, 0:NT], X3[:, :, 1 + t0 + 511: 2 + t0 + 511], AF.Copy))(),
                    reads=[f"X{i}" for i in range(NT)], writes=["xcolb"])
                for s in range(NSL):
                    slot = s % 2
                    first = (b == 0 and f == 0)
                    if first:
                        E("pool", (lambda L=L, s=s, slot=slot: lambda h: h.dma_start(
                            out=WD1.ap()[:, slot * 2048:(slot + 1) * 2048], in_=dt["w1t"][L, s]))(),
                            writes=[f"wd1{slot}"], dma=True)
                        E("sp", (lambda L=L, s=s, slot=slot: lambda h: h.dma_start(
                            out=w1c[L, s], in_=WD1.ap()[:, slot * 2048:(slot + 1) * 2048]))(),
                            reads=[f"wd1{slot}"], writes=[f"w1c{s}"], dma=True)
                    else:
                        q = "sp" if s % 2 == 0 else "pool"
                        E(q, (lambda L=L, s=s, slot=slot: lambda h: h.dma_start(
                            out=WD1.ap()[:, slot * 2048:(slot + 1) * 2048], in_=w1c[L, s]))(),
                            reads=[f"w1c{s}"], writes=[f"wd1{slot}"], dma=True)
                    for halfsl in range(2):
                        pb = (s % 2) * 2 + halfsl
                        for pi in range(NT):
                            E("pe", (lambda s=s, slot=slot, halfsl=halfsl, pi=pi, pb=pb:
                                     lambda h: h.matmul(
                                ps[pb][:, :],
                                WD1.ap()[:, slot * 2048 + pi * 256 + halfsl * 128:
                                         slot * 2048 + pi * 256 + (halfsl + 1) * 128],
                                HBv[:, pi * 512:(pi + 1) * 512],
                                start=(pi == 0), stop=(pi == NT - 1)))(),
                                reads=[f"wd1{slot}", f"HB{pi}"], writes=[f"ps{pb}"])
                    pba = (s % 2) * 2
                    E("act", (lambda L=L, s=s, pba=pba: lambda h: h.activation(
                        GEL[:, :], ps[pba + 1][:, :], AF.Gelu,
                        bias=b1gv[:, L * NSL + s: L * NSL + s + 1]))(),
                        reads=[f"ps{pba + 1}", "c_b1gv"], writes=["gel"])
                    E("dve", (lambda L=L, s=s, pba=pba: lambda h: h.scalar_tensor_tensor(
                        ZB[:, s * 512:(s + 1) * 512], ps[pba][:, :],
                        b1av[:, L * NSL + s: L * NSL + s + 1], GEL[:, :],
                        ALU.add, ALU.mult))(),
                        reads=[f"ps{pba}", "c_b1av", "gel"], writes=[f"ZB{s}"])
                for pi in range(NSL):
                    slot = pi % 2
                    first = (b == 0 and f == 0)
                    if first:
                        E("pool", (lambda L=L, pi=pi, slot=slot: lambda h: h.dma_start(
                            out=WD2.ap()[:, slot * 1024:(slot + 1) * 1024], in_=dt["w2t"][L, pi]))(),
                            writes=[f"wd2{slot}"], dma=True)
                        E("sp", (lambda L=L, pi=pi, slot=slot: lambda h: h.dma_start(
                            out=w2c[L, pi], in_=WD2.ap()[:, slot * 1024:(slot + 1) * 1024]))(),
                            reads=[f"wd2{slot}"], writes=[f"w2c{pi}"], dma=True)
                    else:
                        q = "sp" if pi % 2 == 0 else "pool"
                        E(q, (lambda L=L, pi=pi, slot=slot: lambda h: h.dma_start(
                            out=WD2.ap()[:, slot * 1024:(slot + 1) * 1024], in_=w2c[L, pi]))(),
                            reads=[f"w2c{pi}"], writes=[f"wd2{slot}"], dma=True)
                    for po in range(NT):
                        E("pe", (lambda pi=pi, po=po, slot=slot: lambda h: h.matmul(
                            ps[po][:, :],
                            WD2.ap()[:, slot * 1024 + po * 128: slot * 1024 + (po + 1) * 128],
                            ZB[:, pi * 512:(pi + 1) * 512],
                            start=(pi == 0), stop=(pi == NSL - 1)))(),
                            reads=[f"wd2{slot}", f"ZB{pi}"], writes=[f"ps{po}"])
                for po in range(NT):
                    E("dve", (lambda L=L, po=po, t0=t0: lambda h: h.scalar_tensor_tensor(
                        xsl(po, t0, 512), ps[po][:, :],
                        b2v[:, L * NT + po: L * NT + po + 1],
                        xsl(po, t0, 512), ALU.add, ALU.add))(),
                        reads=[f"ps{po}", "c_b2v", f"X{po}"], writes=[f"X{po}"])
                    E("act", (lambda po=po, t0=t0: lambda h: h.activation(
                        xbsl(po, t0, 512), xsl(po, t0, 512), AF.Copy))(),
                        reads=[f"X{po}"], writes=[f"XB{po}"])

        def phase_final(b):
            BAR(ALL_AR)
            if KCUT >= 4:
                for i in range(NT):
                    E("sp", (lambda b=b, i=i: lambda h: h.dma_start(
                        out=ABUF[:, :], in_=x1sp[b, :, i * T:(i + 1) * T]))(),
                        reads=[f"sp{i}"], writes=["abuf"], dma=True)
                    E("dve", (lambda i=i: lambda h: h.tensor_add(
                        xsl(i, 0, T), xsl(i, 0, T), ABUF[:, :]))(),
                        reads=["abuf", f"X{i}"], writes=[f"X{i}"])
                    E("act", (lambda i=i: lambda h: h.activation(
                        xbsl(i, 0, T), xsl(i, 0, T), AF.Copy))(),
                        reads=[f"X{i}"], writes=[f"XB{i}"])
                for f in range(NF):
                    ln_plain(f)
            for tt in range(KTT):
                cb = tt % 2
                cb3 = CBUF.rearrange("p (t c) -> p t c", c=128)
                if KOUT != "dma":
                    for half in range(2):
                        pb = (4 if cb == 0 else 6) + half
                        for k in range(4):
                            i = half * 4 + k
                            E("pe", (lambda i=i, tt=tt, k=k, pb=pb: lambda h: h.transpose(
                                ps[pb][:, k * 128:(k + 1) * 128],
                                xsl(i, tt * 128, 128), identf[:, :]))(),
                                reads=[f"X{i}", "c_identf"], writes=[f"ps{pb}"])
                        px3 = ps[pb].ap().rearrange("p (t c) -> p t c", c=128)
                        E("dve", (lambda half=half, cb=cb, px3=px3, cb3=cb3: lambda h: h.tensor_copy(
                            cb3[:, cb * 8 + half * 4: cb * 8 + (half + 1) * 4, :], px3[:, :, :]))(),
                            reads=[f"ps{pb}"], writes=[f"cb{cb}"])
                E("sp", (lambda b=b, tt=tt, cb=cb: lambda h: h.dma_start(
                    out=y_out[b, tt * 128:(tt + 1) * 128, :],
                    in_=CBUF[:, cb * 1024:(cb + 1) * 1024]))(),
                    reads=[f"cb{cb}"], writes=[f"out{tt}"], dma=True)

        for b in range(KB):
            BAR(ALL_AR)
            phase_load(b)
            BAR(ALL_AR)
            if KCUT >= 1:
                phase_ssm(b)
            for L in range(NL):
                if KCUT >= 2 + L:
                    phase_layer(b, L)
                if KDBG:
                    for i in range(NT):
                        E("sp", (lambda b=b, L=L, i=i: lambda h: h.dma_start(
                            out=dbg[b, L, :, i * T:(i + 1) * T], in_=xsl(i, 0, T)))(),
                            reads=[f"X{i}"], writes=[f"dbg{i}"], dma=True)
            phase_final(b)

        E("sp", None, reads=[f"out{tt}" for tt in range(KTT)])

        sc.finalize(nc, es)
        block = es.enter_context(nc.Block())

        @block.tensor
        def _(h):
            sc.run_engine("pe", h)

        @block.scalar
        def _(h):
            sc.run_engine("act", h)

        @block.vector
        def _(h):
            sc.run_engine("dve", h)

        @block.gpsimd
        def _(h):
            sc.run_engine("pool", h)

        @block.sync
        def _(h):
            sc.run_engine("sp", h)

    return nc


def _prep_consts():
    half = np.arange(0, DH, 2, dtype=np.float64)
    inv_freq = 1.0 / 10000.0 ** (half / DH)
    t = np.arange(T, dtype=np.float64)
    freqs = t[:, None] * inv_freq[None, :]
    freqs = np.concatenate([freqs, freqs], -1)
    base = (half + 0.4 * DH) / (1.4 * DH)
    power = (t - T // 2) / float(WIN // 2)
    scale = base[None, :] ** power[:, None]
    scale = np.concatenate([scale, scale], -1)
    cos, sin = np.cos(freqs), np.sin(freqs)
    sgn = np.where(np.arange(DH) < DH // 2, -1.0, 1.0)[None, :]
    tile2 = lambda a: np.ascontiguousarray(np.tile(a.T, (2, 1)).astype(np.float32))
    ip = np.arange(128)[None, :]
    jp = np.arange(128)[:, None]
    return dict(
        rotq_c=tile2(cos * scale * 8.0), rotq_s=tile2(sin * scale * sgn * 8.0),
        rotk_c=tile2(cos / scale), rotk_s=tile2(sin / scale * sgn),
        mask_g0=np.where(ip >= jp, 0.0, -1e30).astype(np.float32),
        mask_g4=np.where(ip <= jp, 0.0, -1e30).astype(np.float32),
        identf=np.eye(128, dtype=np.float32),
        epsv=np.tile(np.array([1e-5, 1e-24, -1e-4, 1.0 / D, -1.0, -0.5, 0, 0],
                              dtype=np.float32), (128, 1)),
        rep2c=np.repeat(np.eye(2, dtype=np.float32), 64, axis=1))


def _col_tiles(v, k):
    return np.ascontiguousarray(v.reshape(k, 128).T.astype(np.float32))


def _potile(w):
    din_, dout = w.shape
    a = w.reshape(din_ // 128, 128, dout // 128, 128)
    return np.ascontiguousarray(a.transpose(2, 1, 0, 3).reshape(dout // 128, 128, din_))


def kernel(L0_attn_x, L0_attn_Wd, L0_attn_bd, L0_attn_a_log, L0_attn_b_param,
           L0_attn_Wo, L0_attn_bo, L0_ffn_wqkv, L0_ffn_q_scale, L0_ffn_k_scale,
           L0_ffn_wo, L0_ffn_ln_g, L0_ffn_ln_b, L0_ffn_w1, L0_ffn_b1,
           L0_ffn_w2, L0_ffn_b2):
    consts = _prep_consts()
    f32 = np.float32

    wdt = _potile(np.ascontiguousarray(L0_attn_Wd.T.astype(f32)))
    wot = _potile(np.ascontiguousarray(L0_attn_Wo.T.astype(f32)))
    qkvw = np.stack([_potile(L0_ffn_wqkv[l].astype(f32)) for l in range(NL)])
    wow = np.stack([_potile(L0_ffn_wo[l].astype(f32)) for l in range(NL)])

    w1 = L0_ffn_w1.astype(f32)
    a_half = np.zeros((NL, D, HIDH), f32)
    g_half = np.zeros((NL, D, HIDH), f32)
    a_half[:, :, :HIDR] = w1[:, :, :HIDR]
    g_half[:, :, :HIDR] = w1[:, :, HIDR:]
    A = a_half.reshape(NL, NT, 128, NSL, 128)
    G = g_half.reshape(NL, NT, 128, NSL, 128)
    w1t = np.stack([A.transpose(0, 3, 2, 1, 4), G.transpose(0, 3, 2, 1, 4)], axis=-2)
    w1t = np.ascontiguousarray(w1t.reshape(NL, NSL, 128, NT * 256))

    w2pad = np.zeros((NL, HIDH, D), f32)
    w2pad[:, :HIDR, :] = L0_ffn_w2.astype(f32)
    w2t = np.ascontiguousarray(w2pad.reshape(NL, NSL, 128, D))

    b1 = L0_ffn_b1.astype(f32)
    b1a = np.zeros((NL, HIDH), f32)
    b1g = np.zeros((NL, HIDH), f32)
    b1a[:, :HIDR] = b1[:, :HIDR]
    b1g[:, :HIDR] = b1[:, HIDR:]
    b1av = np.concatenate([_col_tiles(b1a[l], NSL) for l in range(NL)], axis=1)
    b1gv = np.concatenate([_col_tiles(b1g[l], NSL) for l in range(NL)], axis=1)
    b2v = np.concatenate([_col_tiles(L0_ffn_b2[l].astype(f32), NT) for l in range(NL)], axis=1)
    lgv = np.concatenate([_col_tiles(L0_ffn_ln_g[l].astype(f32), NT) for l in range(NL)], axis=1)
    lbv = np.concatenate([_col_tiles(L0_ffn_ln_b[l].astype(f32), NT) for l in range(NL)], axis=1)
    qsv = np.ascontiguousarray(
        np.stack([np.tile(L0_ffn_q_scale[l], 2) for l in range(NL)], axis=1).astype(f32))
    ksv = np.ascontiguousarray(
        np.stack([np.tile(L0_ffn_k_scale[l], 2) for l in range(NL)], axis=1).astype(f32))
    alog = np.ascontiguousarray(
        L0_attn_a_log.reshape(NT, 128, S).transpose(1, 0, 2).reshape(128, 128).astype(f32))
    bpv = np.ascontiguousarray(
        L0_attn_b_param.reshape(NT, 128, S).transpose(1, 0, 2).reshape(128, 128).astype(f32))

    shared = dict(wdt=wdt, wot=wot, qkvw=qkvw, wow=wow, w1t=w1t, w2t=w2t,
                  alog=alog, bpv=bpv,
                  bdv=_col_tiles(L0_attn_bd.astype(f32), NT),
                  bov=_col_tiles(L0_attn_bo.astype(f32), NT),
                  b1av=b1av, b1gv=b1gv, b2v=b2v, lgv=lgv, lbv=lbv,
                  qsv=qsv, ksv=ksv, **consts)
    x = np.ascontiguousarray(L0_attn_x.astype(f32))
    in_maps = []
    for c in range(NCORES):
        m = dict(shared)
        m["x"] = np.ascontiguousarray(x[c * BC:(c + 1) * BC])
        in_maps.append(m)

    nc = build_program()
    res = run_bass_kernel_spmd(nc, in_maps, core_ids=list(range(NCORES)))
    out = np.concatenate([res.results[c]["y"] for c in range(NCORES)], axis=0)
    return out.astype(f32)


if __name__ == "__main__":
    import reference as R
    ins = {k: np.asarray(v) for k, v in R.setup_inputs(0).items()}
    out = kernel(**ins)
    ref = np.asarray(R.reference(**ins))
    err = np.abs(out - ref)
    print("resvar", float((err ** 2).mean() / (ref ** 2).mean()), "max", float(err.max()))


# version 12
# speedup vs baseline: 1.0157x; 1.0157x over previous
import os
import numpy as np
from contextlib import ExitStack

import concourse.bass as bass
import concourse.mybir as mybir
from concourse.bass_utils import run_bass_kernel_spmd

F32 = mybir.dt.float32
BF16 = mybir.dt.bfloat16
AF = mybir.ActivationFunctionType
ALU = mybir.AluOpType

B, T, D = 16, 2048, 1024
NCORES = 8
BC = B // NCORES
NT = D // 128
TT = T // 128
NF = T // 512
S = 16
PAD = 2049
PADB = 2052
H2 = 1024
DH, WIN = 64, 512
HIDR = 2730
HIDH = 2816
NSL = HIDH // 128
NL = 2
NEXP = 2


class Sched:
    ENGS = ("pe", "act", "dve", "pool", "sp")

    def __init__(self):
        self.ops = []
        self.last_w = {}
        self.readers = {}

    def emit(self, eng, fn, reads=(), writes=(), dma=False):
        opid = len(self.ops)
        deps = set()
        for r in reads:
            deps.update(self.last_w.get(r, ()))
            if r.startswith("ps"):
                for rd in self.readers.get(r, ()):
                    if self.ops[rd]["eng"] != eng:
                        deps.add(rd)
        for r in writes:
            deps.update(self.last_w.get(r, ()))
            deps.update(self.readers.get(r, ()))
        self.ops.append(dict(id=opid, eng=eng, fn=fn, dma=dma, deps=deps))
        for r in reads:
            self.readers.setdefault(r, []).append(opid)
        for r in writes:
            self.last_w[r] = [opid]
            self.readers[r] = []
        return opid

    def barrier(self, names):
        deps = set()
        for r in names:
            deps.update(self.last_w.get(r, ()))
            deps.update(self.readers.get(r, ()))
        dl = sorted(deps)
        for r in names:
            self.last_w[r] = dl
            self.readers[r] = []

    def finalize(self, nc, es):
        ops = self.ops
        d_idx = {}
        dc = {e: 0 for e in self.ENGS}
        for op in ops:
            if op["dma"]:
                dc[op["eng"]] += 1
                d_idx[op["id"]] = dc[op["eng"]]
        marked = set()
        last_drain = {e: -1 for e in self.ENGS}
        for op in ops:
            kept = []
            need_drain = False
            for d in op["deps"]:
                p = ops[d]
                same = p["eng"] == op["eng"]
                if same and not p["dma"]:
                    if op["eng"] != "pe" and d > last_drain[op["eng"]]:
                        need_drain = True
                    continue
                kept.append(d)
            if need_drain:
                last_drain[op["eng"]] = op["id"] - 1
            op["drain"] = need_drain
            op["kept"] = kept
            for d in kept:
                if not ops[d]["dma"]:
                    marked.add(d)
        mval = {}
        mc = {e: 0 for e in self.ENGS}
        for op in ops:
            if not op["dma"]:
                if op["id"] in marked:
                    mc[op["eng"]] += 1
                mval[op["id"]] = mc[op["eng"]]
        NS = self.NS = 8
        self.sem_c = {e: es.enter_context(nc.semaphore(f"sc_{e}")) for e in self.ENGS}
        self.sem_d = {e: [es.enter_context(nc.semaphore(f"sd_{e}{k}")) for k in range(NS)]
                      for e in self.ENGS}
        self.marked = marked
        for op in ops:
            if op["dma"]:
                di = d_idx[op["id"]] - 1
                op["slot"] = di % NS
                op["round"] = di // NS
            w = {}
            for d in op["kept"]:
                p = ops[d]
                if p["dma"]:
                    di = d_idx[d] - 1
                    key, v = ("d", p["eng"], di % NS), 16 * (di // NS + 1)
                else:
                    key, v = ("c", p["eng"], 0), mval[d]
                if v > w.get(key, 0):
                    w[key] = v
            op["waits"] = sorted(w.items())

    def run_engine(self, eng_name, handle):
        seen = {}
        for op in self.ops:
            if op["eng"] != eng_name:
                continue
            for (typ, pe, sl), v in op["waits"]:
                if seen.get((typ, pe, sl), -1) >= v:
                    continue
                handle.wait_ge(self.sem_d[pe][sl] if typ == "d" else self.sem_c[pe], v)
                seen[(typ, pe, sl)] = v
            if op.get("drain") and not os.environ.get("KNODRAIN"):
                handle.drain()
            if op["fn"] is None:
                continue
            if op["dma"]:
                sl, rnd = op["slot"], op["round"]
                if rnd > 0 and seen.get(("d", eng_name, sl), -1) < 16 * rnd:
                    handle.wait_ge(self.sem_d[eng_name][sl], 16 * rnd)
                    seen[("d", eng_name, sl)] = 16 * rnd
                instr = op["fn"](handle)
                instr.then_inc(self.sem_d[eng_name][sl], 16)
            else:
                instr = op["fn"](handle)
                if op["id"] in self.marked:
                    instr.then_inc(self.sem_c[eng_name], 1)


def build_program():
    nc = bass.Bass("TRN2", target_bir_lowering=False)
    dt = {}

    def din(name, shape):
        dt[name] = nc.dram_tensor(name, list(shape), F32, kind="ExternalInput").ap()
        return dt[name]

    x_in = din("x", (BC, T, D))
    y_out = nc.dram_tensor("y", [BC, T, D], F32, kind="ExternalOutput").ap()
    x1sp = nc.dram_tensor("x1sp", [BC, 128, NT * T], F32, kind="Internal").ap()
    w1c = nc.dram_tensor("w1c", [NL, NSL, 128, NT * 256], BF16, kind="Internal").ap()
    w2c = nc.dram_tensor("w2c", [NL, NSL, 128, D], BF16, kind="Internal").ap()
    KDBG = bool(os.environ.get("KDBG"))
    KCUT = int(os.environ.get("KCUT", "99"))
    KLOAD = os.environ.get("KLOAD", "full")
    KOUT = os.environ.get("KOUT", "full")
    KTT = int(os.environ.get("KTT", str(TT)))
    KB = int(os.environ.get("KB", str(BC)))
    KPRE = os.environ.get("KPRE", "full")
    dbg = nc.dram_tensor("dbg", [BC, NL, 128, NT * T], F32, kind="Internal").ap() if KDBG else None
    dbg2 = nc.dram_tensor("dbg2", [BC, NL, 128, NT * T], F32, kind="Internal").ap() if KDBG else None
    dbg3 = nc.dram_tensor("dbg3", [12, 128, T], BF16, kind="Internal").ap() if KDBG else None
    dbg4 = nc.dram_tensor("dbg4", [4, 128, T], F32, kind="Internal").ap() if KDBG else None
    dbg5 = nc.dram_tensor("dbg5", [4, 128, 512], F32, kind="Internal").ap() if KDBG else None
    dbg6 = nc.dram_tensor("dbg6", [4, 1, 512], F32, kind="Internal").ap() if KDBG else None

    din("wdt", (NT, 128, NT * 128))
    din("wot", (NT, 128, NT * 128))
    din("qkvw", (NL, 12, 128, NT * 128))
    din("wow", (NL, NT, 128, 512))
    din("w1t", (NL, NSL, 128, NT * 256))
    din("w2t", (NL, NSL, 128, D))
    for nm, sh in [("alog", (128, 128)), ("bpv", (128, 128)), ("bdv", (128, NT)),
                   ("bov", (128, NT)), ("b1av", (128, NL * NSL)), ("b1gv", (128, NL * NSL)),
                   ("b2v", (128, NL * NT)), ("lgv", (128, NL * NT)), ("lbv", (128, NL * NT)),
                   ("qsv", (128, NL)), ("ksv", (128, NL)),
                   ("rotq_c", (128, T)), ("rotq_s", (128, T)),
                   ("rotk_c", (128, T)), ("rotk_s", (128, T)),
                   ("mask_g0", (128, 128)), ("mask_g4", (128, 128)),
                   ("identf", (128, 128)), ("epsv", (128, 8)), ("rep2c", (2, 128))]:
        din(nm, sh)

    es = ExitStack()
    sc = Sched()
    E = sc.emit
    BAR = sc.barrier
    with es:
        sb = lambda n, p, f, d=F32: es.enter_context(nc.sbuf_tensor(n, [p, f], d))
        X = sb("X", 128, NT * PAD)
        XB = sb("XB", 128, NT * PADB, BF16)
        AR = sb("AR", 128, 21248)
        ar = AR.ap()
        arb = ar.bitcast(BF16)
        delta = ar[:, 0:2048]
        u0 = ar[:, 2048:4096]
        abar = ar[:, 4096:6144]
        statb = ar[:, 6144:8192]
        yac = ar[:, 8192:10240]
        ABUF = ar[:, 0:2048]
        BBUF = ar[:, 2048:4096]
        CBUF = ar[:, 4096:6144]
        VT = arb[:, 12288:14336]
        VT2 = arb[:, 16384:18432]
        EXPT = arb[:, 14336:14336 + NEXP * 512]
        TS2 = ar[:, 7680:8192]
        GEL = ar[:, 7680:8192]
        rq_c = arb[:, 16384:18432]
        rq_s = arb[:, 18432:20480]

        HBv = arb[:, 0:4096]
        ZB = arb[:, 4096:4096 + NSL * 512]
        QKY = arb[:, 20480:36864]
        TS1 = ar[:, 18432:18944]
        SQB = arb[:, 37888:38400]
        MUB = ar[0:1, 19200:19712]
        VARB = ar[0:1, 19712:20224]
        INVB = ar[0:1, 20224:20736]
        RCD = ar[0:1, 20736:21248]
        RSQ = ar[0:2, 20736:21248]
        ABAR2 = sb("ABAR2", 128, 1024)
        XCOL = sb("XCOL", 128, NT)
        XCOLB = sb("XCOLB", 128, NT, BF16)
        WD1 = sb("WD1", 128, 4096, BF16)
        WD2 = sb("WD2", 128, 2048, BF16)
        WSS = WD1.ap()[:, 0:2048]
        WOB = WD1.ap()[:, 2048:3072]
        WQK = WD2.ap()
        navec = sb("navec", 128, 128)
        bpv = sb("bpv_s", 128, 128)
        bdv = sb("bdv_s", 128, NT)
        bov = sb("bov_s", 128, NT)
        b1av = sb("b1av_s", 128, NL * NSL)
        b1gv = sb("b1gv_s", 128, NL * NSL)
        b2v = sb("b2v_s", 128, NL * NT)
        lgv = sb("lgv_s", 128, NL * NT)
        lbv = sb("lbv_s", 128, NL * NT)
        qsv = sb("qsv_s", 128, NL)
        ksv = sb("ksv_s", 128, NL)
        rk_c = sb("rk_c_s", 128, T, BF16)
        rk_s = sb("rk_s_s", 128, T, BF16)
        m_g0 = sb("m_g0_s", 128, 128, BF16)
        m_g4 = sb("m_g4_s", 128, 128, BF16)
        identf = sb("identf_s", 128, 128)
        epsv = sb("epsv_s", 128, 8)
        onescol = sb("onescol", 128, 1, BF16)
        onesrow = sb("onesrow", 1, 128)
        sel64 = sb("sel64", 128, 2, BF16)
        rep2 = sb("rep2", 2, 128)
        alog_t = ar[:, 0:128]
        ps = [es.enter_context(nc.psum_tensor(f"ps{i}", [128, 512], F32)) for i in range(8)]

        X3 = X.ap().rearrange("p (t c) -> p t c", c=PAD)
        XB3 = XB.ap().rearrange("p (t c) -> p t c", c=PADB)

        SCAN_N = ["delta", "u0", "abb0", "abb1", "st0", "st1",
                  "y0", "y1", "wss0", "wss1", "c_alog"]
        ATTN_N = ["abuf", "bbuf", "cb0", "cb1", "VT0", "VT1", "ts2", "c_rotq_c", "c_rotq_s"] + \
                 [f"exs{k}" for k in range(NEXP)] + ["wob0", "wob1", "wqk0", "wqk1"]
        W12_N = [f"HB{i}" for i in range(NT)] + [f"ZB{s}" for s in range(NSL)] + \
                ["gel", "wd10", "wd11", "wd20", "wd21"]
        ALL_AR = sorted(set(SCAN_N + ATTN_N + W12_N))

        def xsl(i, t0, n, sh=0):
            o = i * PAD + 1 - sh + t0
            return X[:, o:o + n]

        def xbsl(i, t0, n, sh=0):
            o = i * PADB + 2 - sh + t0
            return XB[:, o:o + n]

        def hx(i, t0, n):
            return xsl(i, t0, n, sh=1 if i >= 4 else 0)

        def hxb(i, t0, n):
            return xbsl(i, t0, n, sh=1 if i >= 4 else 0)

        _cl = [("bdv", bdv), ("bov", bov), ("b1av", b1av), ("b1gv", b1gv),
               ("b2v", b2v), ("lgv", lgv), ("lbv", lbv), ("qsv", qsv),
               ("ksv", ksv),
               ("identf", identf), ("bpv", bpv), ("alog", alog_t),
               ("epsv", None)]
        for nm, dst in [("mask_g0", m_g0), ("mask_g4", m_g4)]:
            E("pool", (lambda d, s: lambda h: h.dma_start(out=d[:, :], in_=s))(dst, dt[nm]),
              writes=[f"c_{nm}"], dma=True)
        if KPRE == "min":
            _cl = [("identf", identf), ("bpv", bpv), ("alog", alog_t), ("epsv", None)]
        for nm, dst in _cl:
            if dst is None:
                dst = epsv
            E("sp", (lambda d, s: lambda h: h.dma_start(out=d[:, :], in_=s))(dst, dt[nm]),
              writes=[f"c_{nm}"], dma=True)
        if KPRE != "min":
            for nm, dst in [("rotk_c", rk_c), ("rotk_s", rk_s)]:
                E("pool", (lambda d, s: lambda h: h.dma_start(out=d[:, :], in_=s))(dst, dt[nm]),
                  writes=[f"c_{nm}"], dma=True)
        def emit_small_memsets():
            E("dve", lambda h: h.memset(onescol[:, :], 1.0), writes=["c_ones"])
            E("dve", lambda h: h.memset(onesrow[:, :], 1.0), writes=["c_onesr"])
            E("dve", lambda h: h.memset(sel64[:, :], 0.0), writes=["c_sel"])
            E("dve", lambda h: h.memset(sel64[0:64, 0:1], 1.0), writes=["c_sel"])
            E("dve", lambda h: h.memset(sel64[64:128, 1:2], 1.0), writes=["c_sel"])
            E("sp", lambda h: h.dma_start(out=rep2[:, :], in_=dt["rep2c"]),
              writes=["c_rep2"], dma=True)
        if KPRE != "nomemset":
            emit_small_memsets()
        E("act", lambda h: h.activation(navec[:, :], alog_t[:, :], AF.Abs),
          reads=["c_alog"], writes=["c_nav"])
        E("act", lambda h: h.activation(navec[:, :], navec[:, :], AF.Exp, scale=epsv[:, 4:5]),
          reads=["c_nav", "c_epsv"], writes=["c_nav"])
        E("act", lambda h: h.activation(navec[:, :], navec[:, :], AF.Ln, bias=1.0),
          reads=["c_nav"], writes=["c_nav"])
        E("act", lambda h: h.activation(abar[:, 0:128], alog_t[:, :], AF.Relu),
          reads=["c_alog"], writes=["abb0"])
        E("dve", lambda h: h.tensor_add(navec[:, :], navec[:, :], abar[:, 0:128]),
          reads=["abb0", "c_nav"], writes=["c_nav"])
        E("dve", lambda h: h.tensor_scalar(navec[:, :], navec[:, :], epsv[:, 4:5], epsv[:, 2:3], ALU.mult, ALU.add),
          reads=["c_nav", "c_epsv"], writes=["c_nav"])
        E("dve", lambda h: h.memset(X3[:, :, 0:1], 0.0), writes=[f"X{i}" for i in range(NT)])
        E("dve", lambda h: h.memset(XB3[:, :, 0:2], 0.0), writes=[f"XB{i}" for i in range(NT)])

        allXB = [f"XB{i}" for i in range(NT)]

        def ln_stats(rhs_list):
            n = len(rhs_list)
            for k, (ap_, nn, off) in enumerate(rhs_list):
                E("pe", (lambda ap_=ap_, k=k, n=n, nn=nn, off=off: lambda h: h.matmul(
                    ps[6][0:1, off:off + nn], onescol[:, :], ap_,
                    start=(k == 0), stop=(k == n - 1)))(),
                    reads=["c_ones"] + allXB + ["xcolb"], writes=["ps6"])
            E("act", lambda h: h.activation(MUB, ps[6][0:1, :], AF.Copy, scale=epsv[0:1, 3:4]),
              reads=["ps6", "c_epsv"], writes=["mu"])
            for k, (ap_, nn, off) in enumerate(rhs_list):
                if nn == 512:
                    E("act", (lambda ap_=ap_: lambda h: h.activation(
                        SQB[:, 0:512], ap_, AF.Square))(),
                        reads=allXB + ["xcolb"], writes=["sqb"])
                else:
                    E("dve", (lambda ap_=ap_, nn=nn, off=off: lambda h: h.tensor_mul(
                        SQB[:, off:off + nn], ap_, ap_))(),
                        reads=allXB + ["xcolb"], writes=["sqb"])
                E("pe", (lambda k=k, nn=nn, off=off, n=n: lambda h: h.matmul(
                    ps[7][0:1, off:off + nn], onescol[:, :], SQB[:, off:off + nn],
                    start=(k == 0), stop=(k == n - 1)))(),
                    reads=["c_ones", "sqb"], writes=["ps7"])
            E("act", lambda h: h.activation(VARB, ps[7][0:1, :], AF.Copy, scale=epsv[0:1, 3:4]),
              reads=["ps7", "c_epsv"], writes=["var"])
            E("dve", lambda h: h.tensor_mul(RCD, MUB, MUB), reads=["mu"], writes=["rcd"])
            E("dve", lambda h: h.tensor_sub(VARB, VARB, RCD),
              reads=["var", "rcd"], writes=["var"])
            E("act", lambda h: h.activation(INVB, VARB, AF.Ln, bias=epsv[0:1, 0:1]),
              reads=["var", "c_epsv"], writes=["inv"])
            E("act", lambda h: h.activation(INVB, INVB, AF.Exp, scale=epsv[0:1, 5:6]),
              reads=["inv", "c_epsv"], writes=["inv"])
            E("dve", lambda h: h.tensor_mul(MUB, MUB, INVB),
              reads=["mu", "inv"], writes=["mu"])

        def ln_replicate():
            E("pe", lambda h: h.matmul(ps[6][:, :], onesrow[:, :], INVB, start=True, stop=True),
              reads=["inv", "c_onesr"], writes=["ps6"])
            E("pe", lambda h: h.matmul(ps[7][:, :], onesrow[:, :], MUB, start=True, stop=True),
              reads=["mu", "c_onesr"], writes=["ps7"])

        def plain_rhs(f):
            return [(xbsl(i, f * 512, 512), 512, 0) for i in range(NT)]

        def ln_plain(f):
            ln_stats(plain_rhs(f))
            ln_replicate()
            t0 = f * 512
            for i in range(NT):
                E("dve", (lambda i=i, t0=t0: lambda h: h.tensor_mul(
                    xsl(i, t0, 512), xsl(i, t0, 512), ps[6][:, :]))(),
                    reads=["ps6", f"X{i}"], writes=[f"X{i}"])
                E("dve", (lambda i=i, t0=t0: lambda h: h.tensor_sub(
                    xsl(i, t0, 512), xsl(i, t0, 512), ps[7][:, :]))(),
                    reads=["ps7", f"X{i}"], writes=[f"X{i}"])
                E("act", (lambda i=i, t0=t0: lambda h: h.activation(
                    xbsl(i, t0, 512), xsl(i, t0, 512), AF.Copy))(),
                    reads=[f"X{i}"], writes=[f"XB{i}"])

        def phase_load(b):
            for tt in range(KTT):
                cb = tt % 2
                E("sp", (lambda tt=tt, cb=cb: lambda h: h.dma_start(
                    out=CBUF[:, cb * 1024:(cb + 1) * 1024],
                    in_=x_in[b, tt * 128:(tt + 1) * 128, :]))(),
                    writes=[f"cb{cb}"], dma=True)
                if KLOAD == "dma":
                    continue
                for half in range(2):
                    pb = (4 if cb == 0 else 6) + half
                    for k in range(4):
                        i = half * 4 + k
                        E("pe", (lambda cb=cb, i=i, k=k, pb=pb: lambda h: h.transpose(
                            ps[pb][:, k * 128:(k + 1) * 128],
                            CBUF[:, cb * 1024 + i * 128: cb * 1024 + (i + 1) * 128],
                            identf[:, :]))(),
                            reads=[f"cb{cb}", "c_identf"], writes=[f"ps{pb}"])
                    px3 = ps[pb].ap().rearrange("p (t c) -> p t c", c=128)
                    E("dve", (lambda tt=tt, half=half, px3=px3: lambda h: h.tensor_copy(
                        X3[:, half * 4:(half + 1) * 4, 1 + tt * 128: 1 + (tt + 1) * 128],
                        px3[:, :, :]))(),
                        reads=[f"ps{pb}"], writes=[f"X{half*4+k}" for k in range(4)])
                    for k in range(4):
                        i2 = half * 4 + k
                        E("act", (lambda tt=tt, i2=i2, k=k, pb=pb: lambda h: h.activation(
                            xbsl(i2, tt * 128, 128),
                            ps[pb][:, k * 128:(k + 1) * 128], AF.Copy))(),
                            reads=[f"ps{pb}"], writes=[f"XB{i2}"])

        def phase_ssm(b):
            for i in range(NT):
                slot = i % 2
                E("pool", (lambda i=i, slot=slot: lambda h: h.dma_start(
                    out=WSS[:, slot * 1024:(slot + 1) * 1024], in_=dt["wdt"][i]))(),
                    writes=[f"wss{slot}"], dma=True)
                for f in range(NF):
                    pb = f % 4
                    for pi in range(NT):
                        E("pe", (lambda i=i, f=f, pi=pi, slot=slot, pb=pb: lambda h: h.matmul(
                            ps[pb][:, :],
                            WSS[:, slot * 1024 + pi * 128: slot * 1024 + (pi + 1) * 128],
                            xbsl(pi, f * 512, 512),
                            start=(pi == 0), stop=(pi == NT - 1)))(),
                            reads=[f"wss{slot}", f"XB{pi}"], writes=[f"ps{pb}"])
                    E("act", (lambda i=i, pb=pb: lambda h: h.activation(
                        TS1[:, :], ps[pb][:, :], AF.Abs, bias=bdv[:, i:i + 1]))(),
                        reads=[f"ps{pb}", "c_bdv"], writes=["ts1"])
                    E("act", (lambda i=i, f=f, pb=pb: lambda h: h.activation(
                        delta[:, f * 512:(f + 1) * 512], ps[pb][:, :], AF.Relu,
                        bias=bdv[:, i:i + 1]))(),
                        reads=[f"ps{pb}", "c_bdv"], writes=["delta"])
                    E("act", lambda h: h.activation(TS1[:, :], TS1[:, :], AF.Exp, scale=epsv[:, 4:5]),
                      reads=["ts1", "c_epsv"], writes=["ts1"])
                    E("act", lambda h: h.activation(TS1[:, :], TS1[:, :], AF.Ln, bias=1.0),
                      reads=["ts1"], writes=["ts1"])
                    E("dve", (lambda f=f: lambda h: h.tensor_add(
                        delta[:, f * 512:(f + 1) * 512], delta[:, f * 512:(f + 1) * 512],
                        TS1[:, :]))(),
                        reads=["ts1", "delta"], writes=["delta"])
                E("pool", (lambda i=i: lambda h: h.tensor_mul(
                    u0[:, :], delta[:, :], xsl(i, 0, T)))(),
                    reads=["delta", f"X{i}"], writes=["u0"])
                bufs = [(abar[:, 0:H2], "abb0"), (abar[:, H2:2 * H2], "abb1"),
                        (ABAR2.ap()[:, 0:H2], "abb2")]
                for j in range(S):
                    c = i * 16 + j
                    for hf in range(2):
                        ab, abn = bufs[(2 * j + hf) % 3]
                        E("act", (lambda c=c, hf=hf, ab=ab: lambda h: h.activation(
                            ab, delta[:, hf * H2:(hf + 1) * H2],
                            AF.Exp, scale=navec[:, c:c + 1]))(),
                            reads=["delta", "c_nav"], writes=[abn])
                        init = 0.0 if hf == 0 else statb[:, H2 - 1:H2]
                        E("dve", (lambda hf=hf, init=init, ab=ab: lambda h: h.tensor_tensor_scan(
                            statb[:, hf * H2:(hf + 1) * H2], ab,
                            u0[:, hf * H2:(hf + 1) * H2], init, ALU.mult, ALU.add))(),
                            reads=[abn, "u0"] + (["st0"] if hf else []),
                            writes=[f"st{hf}"])
                        if j == 0:
                            E("dve", (lambda c=c, hf=hf: lambda h: h.tensor_scalar_mul(
                                yac[:, hf * H2:(hf + 1) * H2], statb[:, hf * H2:(hf + 1) * H2],
                                bpv[:, c:c + 1]))(),
                                reads=[f"st{hf}", "c_bpv"], writes=[f"y{hf}"])
                        else:
                            E("dve", (lambda c=c, hf=hf: lambda h: h.scalar_tensor_tensor(
                                yac[:, hf * H2:(hf + 1) * H2], statb[:, hf * H2:(hf + 1) * H2],
                                bpv[:, c:c + 1], yac[:, hf * H2:(hf + 1) * H2],
                                ALU.mult, ALU.add))(),
                                reads=[f"st{hf}", "c_bpv", f"y{hf}"], writes=[f"y{hf}"])
                E("act", (lambda i=i: lambda h: h.activation(
                    QKY[:, i * T:(i + 1) * T], yac[:, :], AF.Copy))(),
                    reads=["y0", "y1"], writes=[f"QK{i}"])
            for e in range(NT):
                slot = e % 2
                E("pool", (lambda e=e, slot=slot: lambda h: h.dma_start(
                    out=WSS[:, slot * 1024:(slot + 1) * 1024], in_=dt["wot"][e]))(),
                    writes=[f"wss{slot}"], dma=True)
                for f in range(NF):
                    pb = f % 4
                    for pi in range(NT):
                        E("pe", (lambda e=e, f=f, pi=pi, slot=slot, pb=pb: lambda h: h.matmul(
                            ps[pb][:, :],
                            WSS[:, slot * 1024 + pi * 128: slot * 1024 + (pi + 1) * 128],
                            QKY[:, pi * T + f * 512: pi * T + (f + 1) * 512],
                            start=(pi == 0), stop=(pi == NT - 1)))(),
                            reads=[f"wss{slot}", f"QK{pi}"], writes=[f"ps{pb}"])
                    E("dve", (lambda e=e, f=f, pb=pb: lambda h: h.scalar_tensor_tensor(
                        xsl(e, f * 512, 512), ps[pb][:, :], bov[:, e:e + 1],
                        xsl(e, f * 512, 512), ALU.add, ALU.add))(),
                        reads=[f"ps{pb}", "c_bov", f"X{e}"], writes=[f"X{e}"])
                    E("act", (lambda e=e, f=f: lambda h: h.activation(
                        xbsl(e, f * 512, 512), xsl(e, f * 512, 512), AF.Copy))(),
                        reads=[f"X{e}"], writes=[f"XB{e}"])
            for f in range(NF):
                ln_plain(f)
            for i in range(NT):
                E("sp", (lambda b=b, i=i: lambda h: h.dma_start(
                    out=x1sp[b, :, i * T:(i + 1) * T], in_=xsl(i, 0, T)))(),
                    reads=[f"X{i}"], writes=[f"sp{i}"], dma=True)

        def qk_norm_rot(L, pt, kind):
            dtile = pt if kind == "q" else 4 + (pt - 4)
            scv = qsv if kind == "q" else ksv
            rc = rq_c if kind == "q" else rk_c
            rs = rq_s if kind == "q" else rk_s
            rcn = "c_rotq_c" if kind == "q" else "c_rotk_c"
            rsn = "c_rotq_s" if kind == "q" else "c_rotk_s"
            for f in range(NF):
                E("act", (lambda f=f: lambda h: h.activation(
                    SQB[:, :], ABUF[:, f * 512:(f + 1) * 512], AF.Square))(),
                    reads=["abuf"], writes=["sqb"])
                E("pe", lambda h: h.matmul(ps[2][0:2, :], sel64[:, :], SQB[:, :],
                                           start=True, stop=True),
                  reads=["sqb", "c_sel"], writes=["ps2"])
                E("act", lambda h: h.activation(RSQ, ps[2][0:2, :], AF.Ln, bias=epsv[0:2, 1:2]),
                  reads=["ps2", "c_epsv"], writes=["rcd"])
                E("act", lambda h: h.activation(RSQ, RSQ, AF.Exp, scale=epsv[0:2, 5:6]),
                  reads=["rcd", "c_epsv"], writes=["rcd"])
                E("pe", lambda h: h.matmul(ps[3][:, :], rep2[:, :], RSQ,
                                           start=True, stop=True),
                  reads=["rcd", "c_rep2"], writes=["ps3"])
                E("dve", (lambda f=f, L=L, scv=scv: lambda h: h.scalar_tensor_tensor(
                    BBUF[:, f * 512:(f + 1) * 512], ABUF[:, f * 512:(f + 1) * 512],
                    scv[:, L:L + 1], ps[3][:, :], ALU.mult, ALU.mult))(),
                    reads=["abuf", "ps3", "c_qsv", "c_ksv"], writes=["bbuf"])
            for blk in range(4):
                src = blk ^ 1
                E("sp", (lambda blk=blk, src=src: lambda h: h.dma_start(
                    out=ABUF[blk * 32:(blk + 1) * 32, :],
                    in_=BBUF[src * 32:(src + 1) * 32, :]))(),
                    reads=["bbuf"], writes=["abuf"], dma=True)
            for f in range(NF):
                E("dve", (lambda f=f, rc=rc: lambda h: h.tensor_mul(
                    TS1[:, :], BBUF[:, f * 512:(f + 1) * 512], rc[:, f * 512:(f + 1) * 512]))(),
                    reads=["bbuf", rcn], writes=["ts1"])
                E("pool", (lambda f=f, rs=rs: lambda h: h.tensor_mul(
                    TS2[:, :], ABUF[:, f * 512:(f + 1) * 512], rs[:, f * 512:(f + 1) * 512]))(),
                    reads=["abuf", rsn], writes=["ts2"])
                E("dve", (lambda f=f, dtile=dtile: lambda h: h.tensor_add(
                    QKY[:, dtile * T + f * 512: dtile * T + (f + 1) * 512],
                    TS1[:, :], TS2[:, :]))(),
                    reads=["ts1", "ts2"], writes=[f"QK{dtile}"])

        def expt_sl(slot, a, b):
            return EXPT[:, slot * 512 + a: slot * 512 + b]

        def attention_head(L, hd, dump=False):
            hb = (hd % 2) * 64
            qt = hd // 2
            vt_ = VT if hd % 2 == 0 else VT2
            vtn_ = f"VT{hd % 2}"
            for wi in range(4):
                jts = [jt for jt in range(8) if (wi > 0 or jt >= 4)]
                for n, jt in enumerate(jts):
                    its = [it for it in range(4) if 0 <= it - jt + 4 <= 4]
                    c0, c1 = min(its), max(its) + 1
                    pb = [4, 5, 2][n % 3]
                    slot = n % NEXP
                    koff = (wi - 1) * 512 + jt * 128
                    E("pe", (lambda qt=qt, hb=hb, wi=wi, koff=koff, c0=c0, c1=c1, pb=pb: lambda h: h.matmul(
                        ps[pb][:, c0 * 128:c1 * 128],
                        QKY[hb:hb + 64, (4 + qt) * T + koff: (4 + qt) * T + koff + 128],
                        QKY[hb:hb + 64, qt * T + wi * 512 + c0 * 128: qt * T + wi * 512 + c1 * 128],
                        start=True, stop=True))(),
                        reads=[f"QK{4 + qt}", f"QK{qt}"], writes=[f"ps{pb}"])
                    for it in its:
                        g = it - jt + 4
                        if g == 0:
                            E("dve", (lambda it=it, pb=pb: lambda h: h.tensor_add(
                                ps[pb][:, it * 128:(it + 1) * 128],
                                ps[pb][:, it * 128:(it + 1) * 128], m_g0[:, :]))(),
                                reads=[f"ps{pb}", "c_mask_g0"], writes=[f"ps{pb}"])
                        elif g == 4:
                            E("dve", (lambda it=it, pb=pb: lambda h: h.tensor_add(
                                ps[pb][:, it * 128:(it + 1) * 128],
                                ps[pb][:, it * 128:(it + 1) * 128], m_g4[:, :]))(),
                                reads=[f"ps{pb}", "c_mask_g4"], writes=[f"ps{pb}"])
                    if c0 > 0:
                        E("dve", (lambda slot=slot, c0=c0: lambda h: h.memset(
                            expt_sl(slot, 0, c0 * 128), 0.0))(),
                            writes=[f"exs{slot}"])
                    if c1 < 4:
                        E("dve", (lambda slot=slot, c1=c1: lambda h: h.memset(
                            expt_sl(slot, c1 * 128, 512), 0.0))(),
                            writes=[f"exs{slot}"])
                    E("act", (lambda slot=slot, c0=c0, c1=c1, pb=pb: lambda h: h.activation(
                        expt_sl(slot, c0 * 128, c1 * 128),
                        ps[pb][:, c0 * 128:c1 * 128], AF.Exp))(),
                        reads=[f"ps{pb}"], writes=[f"exs{slot}"])
                    ja = koff // 128
                    E("pe", (lambda ja=ja, slot=slot, n=n, vt_=vt_, last=(n == len(jts) - 1): lambda h: h.matmul(
                        ps[6][:, :], vt_[:, ja * 128:(ja + 1) * 128],
                        expt_sl(slot, 0, 512),
                        start=(n == 0), stop=last))(),
                        reads=[vtn_, f"exs{slot}"], writes=["ps6"])
                if hb == 0:
                    E("act", lambda h: h.activation(TS2[64:65, :], ps[6][64:65, :], AF.Copy),
                      reads=["ps6"], writes=["ts2"])
                    E("sp", lambda h: h.dma_start(out=RCD, in_=TS2[64:65, :]),
                      reads=["ts2"], writes=["rcd"], dma=True)
                else:
                    E("act", lambda h: h.activation(RCD, ps[6][0:1, :], AF.Copy),
                      reads=["ps6"], writes=["rcd"])
                E("act", lambda h: h.activation(RCD, RCD, AF.Ln), reads=["rcd"], writes=["rcd"])
                E("act", lambda h: h.activation(RCD, RCD, AF.Exp, scale=epsv[0:1, 4:5]),
                  reads=["rcd", "c_epsv"], writes=["rcd"])
                E("pe", lambda h: h.matmul(ps[7][:, :], onesrow[:, :], RCD,
                                           start=True, stop=True),
                  reads=["rcd", "c_onesr"], writes=["ps7"])
                E("act", (lambda hb=hb: lambda h: h.activation(
                    TS1[hb:hb + 64, :], ps[6][hb:hb + 64, :], AF.Copy))(),
                    reads=["ps6"], writes=["ts1"])
                if dump:
                    E("sp", (lambda hb=hb, wi=wi: lambda h: h.dma_start(
                        out=dbg5[wi, hb:hb + 64, :], in_=TS1[hb:hb + 64, :]))(),
                        reads=["ts1"], writes=[f"d5{wi}"], dma=True)
                    E("sp", (lambda wi=wi: lambda h: h.dma_start(
                        out=dbg6[wi], in_=RCD))(),
                        reads=["rcd"], writes=[f"d6{wi}"], dma=True)
                E("dve", (lambda qt=qt, hb=hb, wi=wi: lambda h: h.tensor_mul(
                    QKY[hb:hb + 64, qt * T + wi * 512: qt * T + (wi + 1) * 512],
                    TS1[hb:hb + 64, :], ps[7][hb:hb + 64, :]))(),
                    reads=["ts1", "ps7"], writes=[f"QK{qt}"])

        def phase_layer(b, L):
            BAR(ALL_AR)
            E("pool", lambda h: h.dma_start(out=rq_c[:, :], in_=dt["rotq_c"]),
              writes=["c_rotq_c"], dma=True)
            E("pool", lambda h: h.dma_start(out=rq_s[:, :], in_=dt["rotq_s"]),
              writes=["c_rotq_s"], dma=True)
            for pt in range(12):
                kind = "q" if pt < 4 else ("k" if pt < 8 else "v")
                slot = pt % 2
                E("pool", (lambda L=L, pt=pt, slot=slot: lambda h: h.dma_start(
                    out=WQK[:, slot * 1024:(slot + 1) * 1024], in_=dt["qkvw"][L, pt]))(),
                    writes=[f"wqk{slot}"], dma=True)
                for f in range(NF):
                    pb = f % 4
                    for pi in range(NT):
                        E("pe", (lambda pt=pt, f=f, pi=pi, slot=slot, pb=pb: lambda h: h.matmul(
                            ps[pb][:, :],
                            WQK[:, slot * 1024 + pi * 128: slot * 1024 + (pi + 1) * 128],
                            hxb(pi, f * 512, 512),
                            start=(pi == 0), stop=(pi == NT - 1)))(),
                            reads=[f"wqk{slot}", f"XB{pi}"], writes=[f"ps{pb}"])
                    if kind == "v":
                        E("act", (lambda f=f, pb=pb: lambda h: h.activation(
                            CBUF[:, f * 512:(f + 1) * 512], ps[pb][:, :], AF.Copy))(),
                            reads=[f"ps{pb}"], writes=[f"cb{f // 2}"])
                    else:
                        E("act", (lambda f=f, pb=pb: lambda h: h.activation(
                            ABUF[:, f * 512:(f + 1) * 512], ps[pb][:, :], AF.Copy))(),
                            reads=[f"ps{pb}"], writes=["abuf"])
                if kind == "v" and KDBG and b == 0 and L == 0:
                    E("sp", (lambda pt=pt: lambda h: h.dma_start(
                        out=dbg4[pt - 8], in_=CBUF[:, :]))(),
                        reads=["cb0", "cb1"], writes=[f"dbg4{pt}"], dma=True)
                if kind != "v":
                    qk_norm_rot(L, pt, kind)
                    if KDBG and b == 0 and L == 0:
                        dti = pt if kind == "q" else 4 + (pt - 4)
                        E("sp", (lambda dti=dti: lambda h: h.dma_start(
                            out=dbg3[dti], in_=QKY[:, dti * T:(dti + 1) * T]))(),
                            reads=[f"QK{dti}"], writes=[f"dbg3{dti}"], dma=True)
                else:
                    for ho in range(2):
                        hd = (pt - 8) * 2 + ho
                        vtb = VT if hd % 2 == 0 else VT2
                        vw = [f"VT{hd % 2}"] + (["c_rotq_c"] if hd % 2 == 1 else [])
                        E("dve", (lambda vtb=vtb: lambda h: h.memset(vtb[:, :], 0.0))(),
                          writes=vw)
                        vt3 = vtb.rearrange("p (t c) -> p t c", c=128)
                        oc = 64 if ho == 0 else 0
                        E("dve", (lambda vt3=vt3, oc=oc: lambda h: h.memset(
                            vt3[:, :, oc:oc + 1], 1.0))(), writes=vw)
                        for grp in range(2):
                            pbt = 0 + grp
                            for k8 in range(8):
                                jt = grp * 8 + k8
                                E("pe", (lambda ho=ho, jt=jt, k8=k8, pbt=pbt: lambda h: h.transpose(
                                    ps[pbt][:, k8 * 64:(k8 + 1) * 64],
                                    CBUF[ho * 64:(ho + 1) * 64, jt * 128:(jt + 1) * 128],
                                    identf[ho * 64:(ho + 1) * 64, ho * 64:(ho + 1) * 64]))(),
                                    reads=["cb0", "cb1", "c_identf"], writes=[f"ps{pbt}"])
                            pv3 = ps[pbt].ap().rearrange("p (t c) -> p t c", c=64)
                            vt64 = vtb.rearrange("p (t c) -> p t c", c=64)
                            du = 0 if ho == 0 else 1
                            E("dve", (lambda grp=grp, du=du, pv3=pv3, vt64=vt64: lambda h: h.tensor_copy(
                                vt64[:, grp * 16 + du: grp * 16 + 16: 2, :], pv3[:, :, :]))(),
                                reads=[f"ps{pbt}"], writes=vw)
                        attention_head(L, hd, dump=(KDBG and b == 0 and L == 0 and hd == 0))
            if KDBG and b == 0 and L == 0:
                for qt_ in range(4):
                    E("sp", (lambda qt_=qt_: lambda h: h.dma_start(
                        out=dbg3[8 + qt_], in_=QKY[:, qt_ * T:(qt_ + 1) * T]))(),
                        reads=[f"QK{qt_}"], writes=[f"dbg3o{qt_}"], dma=True)
            for po in range(NT):
                slot = po % 2
                E("pool", (lambda L=L, po=po, slot=slot: lambda h: h.dma_start(
                    out=WOB[:, slot * 512:(slot + 1) * 512], in_=dt["wow"][L, po]))(),
                    writes=[f"wob{slot}"], dma=True)
                for f in range(NF):
                    pb = (po % 2) * 2 + (f % 2)
                    for pt2 in range(4):
                        E("pe", (lambda pt2=pt2, po=po, f=f, pb=pb, slot=slot: lambda h: h.matmul(
                            ps[pb][:, :],
                            WOB[:, slot * 512 + pt2 * 128: slot * 512 + (pt2 + 1) * 128],
                            QKY[:, pt2 * T + f * 512: pt2 * T + (f + 1) * 512],
                            start=(pt2 == 0), stop=(pt2 == 3)))(),
                            reads=[f"wob{slot}", f"QK{pt2}"], writes=[f"ps{pb}"])
                    E("dve", (lambda po=po, f=f, pb=pb: lambda h: h.tensor_add(
                        xsl(po, f * 512, 512), xsl(po, f * 512, 512), ps[pb][:, :]))(),
                        reads=[f"ps{pb}", f"X{po}"], writes=[f"X{po}"])
                    E("act", (lambda po=po, f=f: lambda h: h.activation(
                        xbsl(po, f * 512, 512), xsl(po, f * 512, 512), AF.Copy))(),
                        reads=[f"X{po}"], writes=[f"XB{po}"])
            if KDBG:
                for i in range(NT):
                    E("sp", (lambda b=b, L=L, i=i: lambda h: h.dma_start(
                        out=dbg2[b, L, :, i * T:(i + 1) * T], in_=xsl(i, 0, T)))(),
                        reads=[f"X{i}"], writes=[f"dbg2{i}"], dma=True)
            BAR(ALL_AR)
            for f in range(NF):
                t0 = f * 512
                rl = []
                for i in range(NT):
                    if i < 4 or f == 0:
                        rl.append((hxb(i, t0, 512), 512, 0))
                    else:
                        rl.append((XCOLB.ap()[:, i:i + 1], 1, 0))
                        rl.append((xbsl(i, t0, 511), 511, 1))
                ln_stats(rl)
                ln_replicate()
                for i in range(NT):
                    if i < 4 or f == 0:
                        E("dve", (lambda i=i, t0=t0: lambda h: h.tensor_mul(
                            TS1[:, :], hx(i, t0, 512), ps[6][:, :]))(),
                            reads=["ps6", f"X{i}", "xcol"], writes=["ts1"])
                    else:
                        E("dve", (lambda i=i: lambda h: h.tensor_mul(
                            TS1[:, 0:1], XCOL.ap()[:, i:i + 1], ps[6][:, 0:1]))(),
                            reads=["ps6", "xcol"], writes=["ts1"])
                        E("dve", (lambda i=i, t0=t0: lambda h: h.tensor_mul(
                            TS1[:, 1:512], xsl(i, t0, 511), ps[6][:, 1:512]))(),
                            reads=["ps6", f"X{i}"], writes=["ts1"])
                    E("dve", lambda h: h.tensor_sub(TS1[:, :], TS1[:, :], ps[7][:, :]),
                      reads=["ps7", "ts1"], writes=["ts1"])
                    E("dve", (lambda i=i, g=L * NT + i: lambda h: h.tensor_scalar(
                        HBv[:, i * 512:(i + 1) * 512], TS1[:, :],
                        lgv[:, g:g + 1], lbv[:, g:g + 1], ALU.mult, ALU.add))(),
                        reads=["ts1", "c_lgv", "c_lbv"], writes=[f"HB{i}"])
                E("dve", (lambda t0=t0: lambda h: h.tensor_copy(
                    XCOL.ap()[:, 0:NT], X3[:, :, 1 + t0 + 511: 2 + t0 + 511]))(),
                    reads=[f"X{i}" for i in range(NT)], writes=["xcol"])
                E("act", (lambda t0=t0: lambda h: h.activation(
                    XCOLB.ap()[:, 0:NT], X3[:, :, 1 + t0 + 511: 2 + t0 + 511], AF.Copy))(),
                    reads=[f"X{i}" for i in range(NT)], writes=["xcolb"])
                for s in range(NSL):
                    slot = s % 2
                    first = (b == 0 and f == 0)
                    if first:
                        E("pool", (lambda L=L, s=s, slot=slot: lambda h: h.dma_start(
                            out=WD1.ap()[:, slot * 2048:(slot + 1) * 2048], in_=dt["w1t"][L, s]))(),
                            writes=[f"wd1{slot}"], dma=True)
                        E("sp", (lambda L=L, s=s, slot=slot: lambda h: h.dma_start(
                            out=w1c[L, s], in_=WD1.ap()[:, slot * 2048:(slot + 1) * 2048]))(),
                            reads=[f"wd1{slot}"], writes=[f"w1c{s}"], dma=True)
                    else:
                        q = "sp"
                        E(q, (lambda L=L, s=s, slot=slot: lambda h: h.dma_start(
                            out=WD1.ap()[:, slot * 2048:(slot + 1) * 2048], in_=w1c[L, s]))(),
                            reads=[f"w1c{s}"], writes=[f"wd1{slot}"], dma=True)
                    for halfsl in range(2):
                        pb = (s % 2) * 2 + halfsl
                        for pi in range(NT):
                            E("pe", (lambda s=s, slot=slot, halfsl=halfsl, pi=pi, pb=pb:
                                     lambda h: h.matmul(
                                ps[pb][:, :],
                                WD1.ap()[:, slot * 2048 + pi * 256 + halfsl * 128:
                                         slot * 2048 + pi * 256 + (halfsl + 1) * 128],
                                HBv[:, pi * 512:(pi + 1) * 512],
                                start=(pi == 0), stop=(pi == NT - 1)))(),
                                reads=[f"wd1{slot}", f"HB{pi}"], writes=[f"ps{pb}"])
                    pba = (s % 2) * 2
                    E("act", (lambda L=L, s=s, pba=pba: lambda h: h.activation(
                        GEL[:, :], ps[pba + 1][:, :], AF.Gelu,
                        bias=b1gv[:, L * NSL + s: L * NSL + s + 1]))(),
                        reads=[f"ps{pba + 1}", "c_b1gv"], writes=["gel"])
                    E("dve", (lambda L=L, s=s, pba=pba: lambda h: h.scalar_tensor_tensor(
                        ZB[:, s * 512:(s + 1) * 512], ps[pba][:, :],
                        b1av[:, L * NSL + s: L * NSL + s + 1], GEL[:, :],
                        ALU.add, ALU.mult))(),
                        reads=[f"ps{pba}", "c_b1av", "gel"], writes=[f"ZB{s}"])
                for pi in range(NSL):
                    slot = pi % 2
                    first = (b == 0 and f == 0)
                    if first:
                        E("pool", (lambda L=L, pi=pi, slot=slot: lambda h: h.dma_start(
                            out=WD2.ap()[:, slot * 1024:(slot + 1) * 1024], in_=dt["w2t"][L, pi]))(),
                            writes=[f"wd2{slot}"], dma=True)
                        E("sp", (lambda L=L, pi=pi, slot=slot: lambda h: h.dma_start(
                            out=w2c[L, pi], in_=WD2.ap()[:, slot * 1024:(slot + 1) * 1024]))(),
                            reads=[f"wd2{slot}"], writes=[f"w2c{pi}"], dma=True)
                    else:
                        q = "sp"
                        E(q, (lambda L=L, pi=pi, slot=slot: lambda h: h.dma_start(
                            out=WD2.ap()[:, slot * 1024:(slot + 1) * 1024], in_=w2c[L, pi]))(),
                            reads=[f"w2c{pi}"], writes=[f"wd2{slot}"], dma=True)
                    for po in range(NT):
                        E("pe", (lambda pi=pi, po=po, slot=slot: lambda h: h.matmul(
                            ps[po][:, :],
                            WD2.ap()[:, slot * 1024 + po * 128: slot * 1024 + (po + 1) * 128],
                            ZB[:, pi * 512:(pi + 1) * 512],
                            start=(pi == 0), stop=(pi == NSL - 1)))(),
                            reads=[f"wd2{slot}", f"ZB{pi}"], writes=[f"ps{po}"])
                for po in range(NT):
                    E("dve", (lambda L=L, po=po, t0=t0: lambda h: h.scalar_tensor_tensor(
                        xsl(po, t0, 512), ps[po][:, :],
                        b2v[:, L * NT + po: L * NT + po + 1],
                        xsl(po, t0, 512), ALU.add, ALU.add))(),
                        reads=[f"ps{po}", "c_b2v", f"X{po}"], writes=[f"X{po}"])
                    E("act", (lambda po=po, t0=t0: lambda h: h.activation(
                        xbsl(po, t0, 512), xsl(po, t0, 512), AF.Copy))(),
                        reads=[f"X{po}"], writes=[f"XB{po}"])

        def phase_final(b):
            BAR(ALL_AR)
            if KCUT >= 4:
                for i in range(NT):
                    E("sp", (lambda b=b, i=i: lambda h: h.dma_start(
                        out=ABUF[:, :], in_=x1sp[b, :, i * T:(i + 1) * T]))(),
                        reads=[f"sp{i}"], writes=["abuf"], dma=True)
                    E("dve", (lambda i=i: lambda h: h.tensor_add(
                        xsl(i, 0, T), xsl(i, 0, T), ABUF[:, :]))(),
                        reads=["abuf", f"X{i}"], writes=[f"X{i}"])
                    E("act", (lambda i=i: lambda h: h.activation(
                        xbsl(i, 0, T), xsl(i, 0, T), AF.Copy))(),
                        reads=[f"X{i}"], writes=[f"XB{i}"])
                for f in range(NF):
                    ln_plain(f)
            for tt in range(KTT):
                cb = tt % 2
                cb3 = CBUF.rearrange("p (t c) -> p t c", c=128)
                if KOUT != "dma":
                    for half in range(2):
                        pb = (4 if cb == 0 else 6) + half
                        for k in range(4):
                            i = half * 4 + k
                            E("pe", (lambda i=i, tt=tt, k=k, pb=pb: lambda h: h.transpose(
                                ps[pb][:, k * 128:(k + 1) * 128],
                                xsl(i, tt * 128, 128), identf[:, :]))(),
                                reads=[f"X{i}", "c_identf"], writes=[f"ps{pb}"])
                        px3 = ps[pb].ap().rearrange("p (t c) -> p t c", c=128)
                        E("dve", (lambda half=half, cb=cb, px3=px3, cb3=cb3: lambda h: h.tensor_copy(
                            cb3[:, cb * 8 + half * 4: cb * 8 + (half + 1) * 4, :], px3[:, :, :]))(),
                            reads=[f"ps{pb}"], writes=[f"cb{cb}"])
                E("sp", (lambda b=b, tt=tt, cb=cb: lambda h: h.dma_start(
                    out=y_out[b, tt * 128:(tt + 1) * 128, :],
                    in_=CBUF[:, cb * 1024:(cb + 1) * 1024]))(),
                    reads=[f"cb{cb}"], writes=[f"out{tt}"], dma=True)

        for b in range(KB):
            BAR(ALL_AR)
            phase_load(b)
            BAR(ALL_AR)
            if KCUT >= 1:
                phase_ssm(b)
            for L in range(NL):
                if KCUT >= 2 + L:
                    phase_layer(b, L)
                if KDBG:
                    for i in range(NT):
                        E("sp", (lambda b=b, L=L, i=i: lambda h: h.dma_start(
                            out=dbg[b, L, :, i * T:(i + 1) * T], in_=xsl(i, 0, T)))(),
                            reads=[f"X{i}"], writes=[f"dbg{i}"], dma=True)
            phase_final(b)

        E("sp", None, reads=[f"out{tt}" for tt in range(KTT)])

        sc.finalize(nc, es)
        block = es.enter_context(nc.Block())

        @block.tensor
        def _(h):
            sc.run_engine("pe", h)

        @block.scalar
        def _(h):
            sc.run_engine("act", h)

        @block.vector
        def _(h):
            sc.run_engine("dve", h)

        @block.gpsimd
        def _(h):
            sc.run_engine("pool", h)

        @block.sync
        def _(h):
            sc.run_engine("sp", h)

    return nc


def _prep_consts():
    half = np.arange(0, DH, 2, dtype=np.float64)
    inv_freq = 1.0 / 10000.0 ** (half / DH)
    t = np.arange(T, dtype=np.float64)
    freqs = t[:, None] * inv_freq[None, :]
    freqs = np.concatenate([freqs, freqs], -1)
    base = (half + 0.4 * DH) / (1.4 * DH)
    power = (t - T // 2) / float(WIN // 2)
    scale = base[None, :] ** power[:, None]
    scale = np.concatenate([scale, scale], -1)
    cos, sin = np.cos(freqs), np.sin(freqs)
    sgn = np.where(np.arange(DH) < DH // 2, -1.0, 1.0)[None, :]
    tile2 = lambda a: np.ascontiguousarray(np.tile(a.T, (2, 1)).astype(np.float32))
    ip = np.arange(128)[None, :]
    jp = np.arange(128)[:, None]
    return dict(
        rotq_c=tile2(cos * scale * 8.0), rotq_s=tile2(sin * scale * sgn * 8.0),
        rotk_c=tile2(cos / scale), rotk_s=tile2(sin / scale * sgn),
        mask_g0=np.where(ip >= jp, 0.0, -1e30).astype(np.float32),
        mask_g4=np.where(ip <= jp, 0.0, -1e30).astype(np.float32),
        identf=np.eye(128, dtype=np.float32),
        epsv=np.tile(np.array([1e-5, 1e-24, -1e-4, 1.0 / D, -1.0, -0.5, 0, 0],
                              dtype=np.float32), (128, 1)),
        rep2c=np.repeat(np.eye(2, dtype=np.float32), 64, axis=1))


def _col_tiles(v, k):
    return np.ascontiguousarray(v.reshape(k, 128).T.astype(np.float32))


def _potile(w):
    din_, dout = w.shape
    a = w.reshape(din_ // 128, 128, dout // 128, 128)
    return np.ascontiguousarray(a.transpose(2, 1, 0, 3).reshape(dout // 128, 128, din_))


def kernel(L0_attn_x, L0_attn_Wd, L0_attn_bd, L0_attn_a_log, L0_attn_b_param,
           L0_attn_Wo, L0_attn_bo, L0_ffn_wqkv, L0_ffn_q_scale, L0_ffn_k_scale,
           L0_ffn_wo, L0_ffn_ln_g, L0_ffn_ln_b, L0_ffn_w1, L0_ffn_b1,
           L0_ffn_w2, L0_ffn_b2):
    consts = _prep_consts()
    f32 = np.float32

    wdt = _potile(np.ascontiguousarray(L0_attn_Wd.T.astype(f32)))
    wot = _potile(np.ascontiguousarray(L0_attn_Wo.T.astype(f32)))
    qkvw = np.stack([_potile(L0_ffn_wqkv[l].astype(f32)) for l in range(NL)])
    wow = np.stack([_potile(L0_ffn_wo[l].astype(f32)) for l in range(NL)])

    w1 = L0_ffn_w1.astype(f32)
    a_half = np.zeros((NL, D, HIDH), f32)
    g_half = np.zeros((NL, D, HIDH), f32)
    a_half[:, :, :HIDR] = w1[:, :, :HIDR]
    g_half[:, :, :HIDR] = w1[:, :, HIDR:]
    A = a_half.reshape(NL, NT, 128, NSL, 128)
    G = g_half.reshape(NL, NT, 128, NSL, 128)
    w1t = np.stack([A.transpose(0, 3, 2, 1, 4), G.transpose(0, 3, 2, 1, 4)], axis=-2)
    w1t = np.ascontiguousarray(w1t.reshape(NL, NSL, 128, NT * 256))

    w2pad = np.zeros((NL, HIDH, D), f32)
    w2pad[:, :HIDR, :] = L0_ffn_w2.astype(f32)
    w2t = np.ascontiguousarray(w2pad.reshape(NL, NSL, 128, D))

    b1 = L0_ffn_b1.astype(f32)
    b1a = np.zeros((NL, HIDH), f32)
    b1g = np.zeros((NL, HIDH), f32)
    b1a[:, :HIDR] = b1[:, :HIDR]
    b1g[:, :HIDR] = b1[:, HIDR:]
    b1av = np.concatenate([_col_tiles(b1a[l], NSL) for l in range(NL)], axis=1)
    b1gv = np.concatenate([_col_tiles(b1g[l], NSL) for l in range(NL)], axis=1)
    b2v = np.concatenate([_col_tiles(L0_ffn_b2[l].astype(f32), NT) for l in range(NL)], axis=1)
    lgv = np.concatenate([_col_tiles(L0_ffn_ln_g[l].astype(f32), NT) for l in range(NL)], axis=1)
    lbv = np.concatenate([_col_tiles(L0_ffn_ln_b[l].astype(f32), NT) for l in range(NL)], axis=1)
    qsv = np.ascontiguousarray(
        np.stack([np.tile(L0_ffn_q_scale[l], 2) for l in range(NL)], axis=1).astype(f32))
    ksv = np.ascontiguousarray(
        np.stack([np.tile(L0_ffn_k_scale[l], 2) for l in range(NL)], axis=1).astype(f32))
    alog = np.ascontiguousarray(
        L0_attn_a_log.reshape(NT, 128, S).transpose(1, 0, 2).reshape(128, 128).astype(f32))
    bpv = np.ascontiguousarray(
        L0_attn_b_param.reshape(NT, 128, S).transpose(1, 0, 2).reshape(128, 128).astype(f32))

    shared = dict(wdt=wdt, wot=wot, qkvw=qkvw, wow=wow, w1t=w1t, w2t=w2t,
                  alog=alog, bpv=bpv,
                  bdv=_col_tiles(L0_attn_bd.astype(f32), NT),
                  bov=_col_tiles(L0_attn_bo.astype(f32), NT),
                  b1av=b1av, b1gv=b1gv, b2v=b2v, lgv=lgv, lbv=lbv,
                  qsv=qsv, ksv=ksv, **consts)
    x = np.ascontiguousarray(L0_attn_x.astype(f32))
    in_maps = []
    for c in range(NCORES):
        m = dict(shared)
        m["x"] = np.ascontiguousarray(x[c * BC:(c + 1) * BC])
        in_maps.append(m)

    nc = build_program()
    res = run_bass_kernel_spmd(nc, in_maps, core_ids=list(range(NCORES)))
    out = np.concatenate([res.results[c]["y"] for c in range(NCORES)], axis=0)
    return out.astype(f32)


if __name__ == "__main__":
    import reference as R
    ins = {k: np.asarray(v) for k, v in R.setup_inputs(0).items()}
    out = kernel(**ins)
    ref = np.asarray(R.reference(**ins))
    err = np.abs(out - ref)
    print("resvar", float((err ** 2).mean() / (ref ** 2).mean()), "max", float(err.max()))
